# version 01 (final)
# baseline (speedup 1.0000x reference)
import math
import numpy as np
import ml_dtypes
from contextlib import ExitStack

import concourse.bass as bass
import concourse.mybir as mybir
from concourse.bass_utils import run_bass_kernel_spmd

F32 = mybir.dt.float32
BF16 = mybir.dt.bfloat16
AF = mybir.ActivationFunctionType
ALU = mybir.AluOpType
AX = mybir.AxisListType

T = 2048
D = 1024
NEG = -1.0e30

ENGINES = ("tensor", "vector", "scalar", "gpsimd", "sync")


class Prog:
    def __init__(self, nc, stack):
        self.nc = nc
        self.q = {e: [] for e in ENGINES}
        self.sem = {}
        self.last_wait = {}
        for name in ("t", "v", "a", "p", "dw", "dx", "ds"):
            h = stack.enter_context(nc.semaphore("sem_" + name))
            self.sem[name] = [h, 0]

    def count(self, s):
        return self.sem[s][1]

    def _wait_list(self, eng, waits):
        out = []
        for w in waits:
            if isinstance(w, tuple):
                s, n = w
            else:
                s, n = w, self.sem[w][1]
            if n <= 0:
                continue
            key = (eng, s)
            if self.last_wait.get(key, -1) >= n:
                continue
            self.last_wait[key] = n
            out.append((self.sem[s][0], n))
        return out

    def op(self, eng, fn, *, inc=None, inck=1, waits=()):
        wl = self._wait_list(eng, waits)
        if inc is not None:
            self.sem[inc][1] += inck
            h = self.sem[inc][0]
        else:
            h = None
        k = inck

        def run(e, wl=wl, fn=fn, h=h, k=k):
            for sh, n in wl:
                e.wait_ge(sh, n)
            instr = fn(e)
            if h is not None:
                assert instr is not None
                instr.then_inc(h, k)
        self.q[eng].append(run)

    def vred(self, fn):
        self.op("vector", fn, inc="v")
        self.wait("vector", "v")

    vop = vred

    def aop(self, fn):
        self.op("scalar", fn, inc="a")
        self.wait("scalar", "a")

    def wait(self, eng, *sems):
        wl = self._wait_list(eng, sems)
        if not wl:
            return

        def run(e, wl=wl):
            for sh, n in wl:
                e.wait_ge(sh, n)
        self.q[eng].append(run)

    def dma(self, qeng, dst, src, *, sem, waits=()):
        self.op(qeng, lambda e: e.dma_start(dst, src), inc=sem, inck=16,
                waits=waits)

    def emit(self, block):
        for name in ENGINES:
            ops = self.q[name]

            def section(e, ops=ops):
                for f in ops:
                    f(e)
            getattr(block, name)(section)


def mm_acc(P, psum_ap, tiles, *, inc="t", waits=()):
    n = len(tiles)
    for j, (l, r) in enumerate(tiles):
        P.op("tensor",
             lambda e, l=l, r=r, s=(j == 0), st=(j == n - 1): e.matmul(
                 psum_ap, l, r, start=s, stop=st),
             inc=inc if j == n - 1 else None,
             waits=waits if j == 0 else ())


class WStream:

    def __init__(self, P, region_ap, slot_cols, nslots, sem="dw"):
        self.P = P
        self.slot_cols = slot_cols
        self.slots = [region_ap[:, i * slot_cols:(i + 1) * slot_cols]
                      for i in range(nslots)]
        self.nslots = nslots
        self.sem = sem
        self.n = 0
        g = ("t", P.count("t"))
        self.guard = [g] * nslots

    def load(self, src3, kn, on):
        i = self.n % self.nslots
        self.n += 1
        cols = kn * on
        assert cols <= self.slot_cols, (cols, self.slot_cols)
        waits = (self.guard[i],) if self.guard[i] is not None else ()
        flat = self.slots[i][:, 0:cols]
        dst3 = flat.rearrange("p (c o) -> p c o", o=on)
        self.P.dma("gpsimd", dst3, src3, sem=self.sem, waits=waits)
        return flat, self.P.count(self.sem)

    def release_last(self, k=1):
        for j in range(1, k + 1):
            i = (self.n - j) % self.nslots
            self.guard[i] = ("t", self.P.count("t"))


def w_slab(ap, kc0, kn, o0, on):
    r = ap.rearrange("(c p) o -> p c o", p=128)
    return r[:, kc0:kc0 + kn, o0:o0 + on]



class MK:
    def __init__(self):
        self.nc = bass.Bass("TRN2", target_bir_lowering=False)
        self.stack = ExitStack()
        self.dram = {}

    def din(self, name, shape, dtype=F32):
        h = self.nc.dram_tensor(name, list(shape), dtype, kind="ExternalInput")
        self.dram[name] = h.ap()
        return self.dram[name]

    def alloc(self):
        nc, st = self.nc, self.stack
        e = st.enter_context
        self.X = e(nc.sbuf_tensor("X", [128, 16384], F32))
        self.B0 = e(nc.sbuf_tensor("B0", [128, 16384], BF16))
        self.B1 = e(nc.sbuf_tensor("B1", [128, 16384], BF16))
        self.B2 = e(nc.sbuf_tensor("B2", [128, 16384], BF16))
        self.SP = e(nc.sbuf_tensor("SP", [128, 4096], F32))
        self.SM = e(nc.sbuf_tensor("SM", [128, 2048], F32))
        self.WR = e(nc.sbuf_tensor("WR", [128, 5632], BF16))
        self.identb = e(nc.sbuf_tensor("identb", [128, 128], BF16))
        self.identf = e(nc.sbuf_tensor("identf", [128, 128], F32))
        self.iota_j = e(nc.sbuf_tensor("iota_j", [128, 128], F32))
        self.iota_p = e(nc.sbuf_tensor("iota_p", [128, 128], F32))
        self.onesb_col = e(nc.sbuf_tensor("onesb_col", [128, 8], BF16))
        self.onesf_col = e(nc.sbuf_tensor("onesf_col", [128, 8], F32))
        self.onesf_row = e(nc.sbuf_tensor("onesf_row", [1, 128], F32))
        self.onesb_row = e(nc.sbuf_tensor("onesb_row", [1, 128], BF16))
        self.cmask = e(nc.sbuf_tensor("cmask", [128, 2048], BF16))
        self.rot = e(nc.sbuf_tensor("rot", [128, 128], BF16))
        self.decay = e(nc.sbuf_tensor("decay", [128, 640], BF16))
        self.glatri = e(nc.sbuf_tensor("glatri", [128, 128], BF16))
        self.cols = e(nc.sbuf_tensor("cols", [128, 64], F32))
        self.sel2 = e(nc.sbuf_tensor("sel2", [128, 2], BF16))
        self.sel8m2 = e(nc.sbuf_tensor("sel8m2", [64, 64], BF16))
        self.sel8p1 = e(nc.sbuf_tensor("sel8p1", [64, 64], BF16))
        self.bias = e(nc.sbuf_tensor("biascols", [128, 384], F32))
        self.smallw = e(nc.sbuf_tensor("smallw", [128, 384], BF16))
        self.pb = [e(nc.psum_tensor(f"pbank{i}", [128, 512], F32))
                   for i in range(8)]
        self.pbb = [p.bitcast(BF16) for p in self.pb]
        self.xspill = self.nc.dram_tensor("xspill", [1024, 2048], F32).ap()
        self.sspill = self.nc.dram_tensor("sspill", [1024, 2048], BF16).ap()
        self.P = Prog(self.nc, st)
        self.Xb = self.X.bitcast(BF16)
        self.SMb = self.SM.bitcast(BF16)
        self.SPb = self.SP.bitcast(BF16)
        self.B1f = self.B1.bitcast(F32)
        self.B2f = self.B2.bitcast(F32)
        self.rope_cos = self.SPb[:, 0:2048]
        self.rope_sin = self.SPb[:, 2048:4096]

    def xv(self, c, t0=0, tn=T):
        return self.X[:, c * 2048 + t0: c * 2048 + t0 + tn]

    @staticmethod
    def ch(slot, c, t0=0, tn=T):
        return slot[:, c * 2048 + t0: c * 2048 + t0 + tn]

    def tap1(self, name, ap):
        if name not in self.debug_taps:
            return
        P = self.P
        P.wait("gpsimd", "v", "a", "t", "p")
        P.dma("gpsimd", self.debug_taps[name][:, :], ap, sem="ds")
        for eng in ("tensor", "vector", "scalar"):
            P.wait(eng, "ds")

    def tap(self, name, views, cols=T):
        if name not in self.debug_taps:
            return
        P = self.P
        P.wait("gpsimd", "v", "a", "t", "p")
        ap = self.debug_taps[name]
        for c, v in enumerate(views):
            P.dma("gpsimd", ap[c * 128:(c + 1) * 128, 0:cols], v, sem="ds")
        for eng in ("tensor", "vector", "scalar"):
            P.wait(eng, "ds")

    def emit_consts(self):
        P = self.P
        P.op("gpsimd", lambda e: e.iota(self.iota_j[:], [[1, 128]], base=0,
             channel_multiplier=0, allow_small_or_imprecise_dtypes=True))
        P.op("gpsimd", lambda e: e.iota(self.iota_p[:], [[0, 128]], base=0,
             channel_multiplier=1, allow_small_or_imprecise_dtypes=True))
        P.op("gpsimd", lambda e: e.memset(self.onesb_col[:], 1.0))
        P.op("gpsimd", lambda e: e.memset(self.onesf_col[:], 1.0))
        P.op("gpsimd", lambda e: e.memset(self.onesf_row[:], 1.0))
        P.op("gpsimd", lambda e: e.memset(self.onesb_row[:], 1.0))
        P.op("gpsimd", lambda e: e.memset(self.cols[:, 24:25], 1e-5))
        P.op("gpsimd", lambda e: e.iota(self.SM[:, 0:512], [[1, 512]], base=0,
             channel_multiplier=-1, allow_small_or_imprecise_dtypes=True),
             inc="p")
        P.wait("vector", "p")
        P.op("vector", lambda e: e.tensor_tensor(
            self.identb[:], self.iota_j[:], self.iota_p[:], ALU.is_equal))
        P.op("vector", lambda e: e.tensor_tensor(
            self.identf[:], self.iota_j[:], self.iota_p[:], ALU.is_equal))
        P.op("vector", lambda e: e.tensor_tensor(
            self.glatri[:], self.iota_p[:], self.iota_j[:], ALU.is_ge))
        for o in range(4):
            dst = self.cmask[:, o * 512:(o + 1) * 512]
            P.op("vector", lambda e, dst=dst, o=o: e.tensor_scalar(
                dst, self.SM[:, 0:512], float(o * 128), NEG, ALU.is_gt,
                ALU.mult))
        P.op("vector", lambda e: e.tensor_scalar(
            self.sel2[:, 0:1], self.iota_p[:, 0:1], 64.0, 8.0, ALU.is_lt,
            ALU.mult))
        P.op("vector", lambda e: e.tensor_scalar(
            self.sel2[:, 1:2], self.iota_p[:, 0:1], 64.0, 8.0, ALU.is_ge,
            ALU.mult))
        for ei in range(8):
            P.op("vector", lambda e, ei=ei: e.tensor_scalar(
                self.sel8m2[:, ei * 8:(ei + 1) * 8],
                self.iota_j[0:64, 0:8], float(ei), -2.0, ALU.is_equal,
                ALU.mult))
            P.op("vector", lambda e, ei=ei: e.tensor_scalar(
                self.sel8p1[:, ei * 8:(ei + 1) * 8],
                self.iota_j[0:64, 0:8], float(ei), None, ALU.is_equal,
                ALU.bypass))
        for d in range(5):
            dst = self.decay[:, d * 128:(d + 1) * 128]
            P.op("vector", lambda e, dst=dst: e.tensor_tensor(
                dst, self.iota_p[:], self.iota_j[:], ALU.subtract))
            P.op("vector", lambda e, dst=dst, d=d: e.tensor_scalar(
                dst, dst, float(128 * (d - 2)), None, ALU.add, ALU.bypass),
                inc="v" if d == 4 else None)
        P.wait("scalar", "v")
        for d in range(5):
            dst = self.decay[:, d * 128:(d + 1) * 128]
            P.op("scalar", lambda e, dst=dst: e.activation(dst, dst, AF.Abs))
        for d in range(5):
            dst = self.decay[:, d * 128:(d + 1) * 128]
            P.op("scalar", lambda e, dst=dst: e.activation(
                dst, dst, AF.Exp, scale=-0.1), inc="a" if d == 4 else None)
        P.dma("sync", self.rot[:], self.dram["hc_rot"], sem="dx")
        P.dma("sync", self.bias[:], self.dram["hc_biascols"], sem="dx")

    def emit_ln(self, x_views, out_views, sq_region, gamma_col=None,
                beta_col=None, eps=1e-5):
        P = self.P
        P.wait("scalar", "v", "a")
        P.wait("tensor", "v", "a")
        P.wait("vector", "a")
        for c in range(8):
            P.op("scalar", lambda e, c=c: e.activation(
                sq_region[:, c * 2048:(c + 1) * 2048], x_views[c], AF.Square),
                inc="a" if c == 7 else None)
        for nt in range(4):
            tiles = [(self.onesf_col[:, 0:1],
                      x_views[c][:, nt * 512:(nt + 1) * 512]) for c in range(8)]
            mm_acc(P, self.pb[nt][0:1, :], tiles)
        P.wait("tensor", "a")
        for nt in range(4):
            tiles = [(self.onesb_col[:, 0:1],
                      sq_region[:, c * 2048 + nt * 512: c * 2048 + (nt + 1) * 512])
                     for c in range(8)]
            mm_acc(P, self.pb[4 + nt][0:1, :], tiles)
        mean = self.SP[0:1, 0:2048]
        var = self.SP[0:1, 2048:4096]
        rstd = var
        nmr = mean
        P.wait("vector", "t")
        for nt in range(4):
            sl = slice(nt * 512, (nt + 1) * 512)
            P.op("vector", lambda e, nt=nt, sl=sl: e.tensor_scalar(
                mean[:, sl], self.pb[nt][0:1, :], 1.0 / D, None, ALU.mult,
                ALU.bypass), inc="v" if nt == 3 else None)
        P.wait("scalar", "v")
        P.op("scalar", lambda e: e.activation(var[:], mean[:], AF.Square),
             inc="a")
        P.wait("vector", "a")
        for nt in range(4):
            sl = slice(nt * 512, (nt + 1) * 512)
            P.op("vector", lambda e, nt=nt, sl=sl: e.scalar_tensor_tensor(
                var[:, sl], self.pb[4 + nt][0:1, :], 1.0 / D, var[:, sl],
                ALU.mult, ALU.subtract), inc="v" if nt == 3 else None)
        P.wait("scalar", "v", "p")
        P.op("scalar", lambda e: e.activation(rstd[:], var[:], AF.Ln,
             bias=self.cols[0:1, 24:25]))
        P.op("scalar", lambda e: e.activation(rstd[:], rstd[:], AF.Exp,
             scale=-0.5), inc="a")
        P.wait("vector", "a")
        P.op("vector", lambda e: e.scalar_tensor_tensor(
            nmr[:], mean[:], -1.0, rstd[:], ALU.mult, ALU.mult), inc="v")
        P.wait("tensor", "v", "a")
        for nt in range(4):
            sl = slice(nt * 512, (nt + 1) * 512)
            P.op("tensor", lambda e, sl=sl, nt=nt: e.matmul(
                self.pb[nt][:, :], self.onesf_row[:], rstd[:, sl],
                start=True, stop=True))
            P.op("tensor", lambda e, sl=sl, nt=nt: e.matmul(
                self.pb[4 + nt][:, :], self.onesf_row[:], nmr[:, sl],
                start=True, stop=True), inc="t" if nt == 3 else None)
        P.wait("vector", "t")
        RSTD = self.SP[:, 0:2048]
        NMR = self.SP[:, 2048:4096]
        for nt in range(4):
            sl = slice(nt * 512, (nt + 1) * 512)
            P.op("vector", lambda e, nt=nt, sl=sl: e.tensor_copy(
                RSTD[:, sl], self.pb[nt][:, :]))
            P.op("vector", lambda e, nt=nt, sl=sl: e.tensor_copy(
                NMR[:, sl], self.pb[4 + nt][:, :]),
                inc="v" if nt == 3 else None)
        tmp = self.SM[:, 0:2048]
        for c in range(8):
            P.op("vector", lambda e, c=c: e.tensor_tensor(
                tmp[:], x_views[c], RSTD, ALU.mult))
            if gamma_col is None:
                P.op("vector", lambda e, c=c: e.tensor_tensor(
                    out_views[c], tmp[:], NMR, ALU.add),
                    inc="v" if c == 7 else None)
            else:
                P.op("vector", lambda e, c=c: e.tensor_tensor(
                    tmp[:], tmp[:], NMR, ALU.add))
                P.op("vector", lambda e, c=c: e.tensor_scalar(
                    out_views[c], tmp[:], gamma_col[:, c:c + 1],
                    beta_col[:, c:c + 1], ALU.mult, ALU.add),
                    inc="v" if c == 7 else None)
        P.wait("tensor", "v")
        P.wait("scalar", "v")

    def project(self, w_name, src_views, out_cb, ws, oc_range=None,
                o_total=D, kc_range=None):
        P = self.P
        w_ap = self.dram[w_name] if isinstance(w_name, str) else w_name
        kcs = list(kc_range if kc_range is not None else range(8))
        ocs = list(oc_range if oc_range is not None else range(o_total // 128))
        for oc in ocs:
            slab, wn = ws.load(w_slab(w_ap, kcs[0], len(kcs), oc * 128, 128),
                               len(kcs), 128)
            for nt in range(4):
                pbank = self.pb[2 + nt % 2]
                tiles = [(slab[:, j * 128:(j + 1) * 128],
                          src_views[j][:, nt * 512:(nt + 1) * 512])
                         for j in range(len(kcs))]
                mm_acc(P, pbank[:, :], tiles,
                       waits=(("dw", wn),) if nt == 0 else ())
                out_cb(oc, nt, pbank)
                P.wait("tensor", "a", "v")
            ws.release_last()

    def spill_x(self):
        P = self.P
        xr = self.xspill.rearrange("(c p) t -> p c t", p=128)
        P.wait("sync", "v", "t")
        P.dma("sync", xr, self.X[:].rearrange("p (c t) -> p c t", t=2048),
              sem="dx")
        self.x_spilled_at = P.count("dx")

    def gate_top2(self, e_sb, cw_out, thr=None):
        P = self.P
        msc = self.cols[:, 16:24]
        mask = self.SM[:, 1608:1616]
        ew = self.SM[:, 1616:1624]
        P.vop(lambda e: e.reduce_max(msc[:, 2:3], e_sb[:], axis=AX.X))
        P.vop(lambda e: e.tensor_scalar(
            mask[:], e_sb[:], msc[:, 2:3], None, ALU.is_lt, ALU.bypass))
        P.vop(lambda e: e.tensor_tensor(ew[:], e_sb[:], mask[:], ALU.mult))
        P.vop(lambda e: e.reduce_max(msc[:, 3:4], ew[:], axis=AX.X))
        P.vop(lambda e: e.tensor_scalar(
            mask[:], e_sb[:], msc[:, 3:4], None, ALU.is_ge, ALU.bypass))
        if thr is not None:
            P.vop(lambda e: e.reduce_sum(msc[:, 4:5], e_sb[:], axis=AX.X))
            P.vop(lambda e: e.tensor_scalar(
                msc[:, 5:6], msc[:, 4:5], float(thr), None, ALU.mult,
                ALU.bypass))
            P.vop(lambda e: e.tensor_scalar(
                ew[:], e_sb[:], msc[:, 5:6], None, ALU.is_gt, ALU.bypass))
            P.vop(lambda e: e.tensor_tensor(mask[:], mask[:], ew[:],
                                            ALU.mult))
        P.vop(lambda e: e.tensor_tensor(ew[:], e_sb[:], mask[:], ALU.mult))
        P.vop(lambda e: e.reduce_sum(msc[:, 6:7], ew[:], axis=AX.X))
        P.vop(lambda e: e.tensor_scalar(
            msc[:, 6:7], msc[:, 6:7], 1e-9, None, ALU.max, ALU.bypass))
        P.vop(lambda e: e.reciprocal(msc[:, 7:8], msc[:, 6:7]))
        P.op("vector", lambda e: e.tensor_scalar(
            cw_out[:], ew[:], msc[:, 7:8], None, ALU.mult, ALU.bypass),
            inc="v")

    def gate_cwf(self, logits_cb, thr=None, nexp=8):
        P = self.P
        CWb = self.SMb[:, 3072:3200]
        e_sb = self.SM[:, 1600:1608]
        msc = self.cols[:, 16:24]
        for tt in range(16):
            logits_cb(tt, self.pb[7])
            P.wait("vector", "t")
            P.vred(lambda e: e.reduce_max(
                msc[:, 0:1], self.pb[7][:, 0:nexp], axis=AX.X))
            P.op("vector", lambda e: e.tensor_scalar(
                msc[:, 1:2], msc[:, 0:1], -1.0, None, ALU.mult, ALU.bypass),
                inc="v")
            P.wait("scalar", "v")
            P.op("scalar", lambda e: e.activation(
                e_sb[:, 0:nexp], self.pb[7][:, 0:nexp], AF.Exp,
                bias=msc[:, 1:2]), inc="a")
            P.wait("vector", "a")
            self.gate_top2(e_sb[:, 0:nexp],
                           CWb[:, tt * 8:tt * 8 + nexp], thr=thr)
            P.wait("tensor", "v")
        return CWb

    def bw_from_cw(self, CWb, ei, tt0, ntt, BW):
        P = self.P
        rowx = self.SMb[:, 3248:3760]
        for g in range(0, ntt, 4):
            gn = min(4, ntt - g)
            P.wait("tensor", "v")
            for j in range(gn):
                tt = tt0 + g + j
                P.op("tensor", lambda e, tt=tt, j=j: e.matmul(
                    self.pb[6][0:1, j * 128:(j + 1) * 128],
                    CWb[:, tt * 8 + ei: tt * 8 + ei + 1],
                    self.identb[:], start=True, stop=True,
                    skip_group_check=True),
                    inc="t" if j == gn - 1 else None)
            P.wait("vector", "t")
            P.op("vector", lambda e, gn=gn: e.tensor_copy(
                rowx[0:1, 0:gn * 128], self.pb[6][0:1, 0:gn * 128]), inc="v")
            P.wait("tensor", "v")
            P.op("tensor", lambda e, gn=gn: e.matmul(
                self.pb[6][:, 0:gn * 128], self.onesb_row[:],
                rowx[0:1, 0:gn * 128], start=True, stop=True), inc="t")
            P.wait("vector", "t")
            P.op("vector", lambda e, g=g, gn=gn: e.tensor_copy(
                BW[:, g * 128:(g + gn) * 128], self.pb[6][:, 0:gn * 128]),
                inc="v")

    def acc_ap(self, oc, tb, t0=0, tn=1024):
        return self.X[:, oc * 2048 + tb * 1024 + t0:
                      oc * 2048 + tb * 1024 + t0 + tn]

    def combine_tb(self, tb, s_slot, xold_f32_region, tbw=1024,
                   xold_views=None):
        P = self.P
        xr = self.xspill.rearrange("(c p) t -> p c t", p=128)
        xold = xold_f32_region
        P.wait("sync", "v", "a", "t", ("dx", self.x_spilled_at))
        if xold_views is None:
            dst3 = xold.rearrange("p (c t) -> p c t", t=tbw)
        else:
            dst3, xold = xold_views
        P.dma("sync", dst3, xr[:, :, tb * tbw:(tb + 1) * tbw], sem="dx")
        gtmp = self.SM[:, 0:1024]
        P.wait("scalar", "v", ("dx", P.count("dx")))
        P.wait("vector", ("dx", P.count("dx")))
        for oc in range(8):
            acc = self.acc_ap(oc, tb, 0, tbw)
            P.op("scalar", lambda e, acc=acc: e.activation(
                gtmp[:, 0:tbw], acc, AF.Silu), inc="a")
            P.wait("vector", "a")
            s_ap = self.ch(s_slot, oc, tb * tbw, tbw)
            P.op("vector", lambda e, s_ap=s_ap: e.tensor_tensor(
                gtmp[:, 0:tbw], s_ap, gtmp[:, 0:tbw], ALU.mult))
            xo = xold[oc] if isinstance(xold, list) else \
                xold[:, oc * tbw:(oc + 1) * tbw]
            dst = self.xv(oc, tb * tbw, tbw)
            P.op("vector", lambda e, xo=xo, dst=dst: e.tensor_tensor(
                dst, xo, gtmp[:, 0:tbw], ALU.add), inc="v")
            P.wait("scalar", "v")

    def proj_v_tok(self, wname, dst_slot, dst_base, src_slot=None):
        P = self.P
        wv = self.dram[wname] if isinstance(wname, str) else wname
        src = src_slot if src_slot is not None else self.B0
        P.wait("scalar", ("dx", self.x_spilled_at))
        P.wait("vector", ("dx", self.x_spilled_at))
        ws_v = WStream(P, self.WR[:, 0:4096], 512, 8)
        for ot in range(2):
            vslabs = [ws_v.load(w_slab(wv, c, 1, ot * 512, 512), 1, 512)
                      for c in range(8)]
            for kc in range(16):
                pbank = self.pb[kc % 2]
                tiles = [(self.ch(src, c, kc * 128, 128), s_ap)
                         for c, (s_ap, wn) in enumerate(vslabs)]
                mm_acc(P, pbank[:, :], tiles,
                       waits=(("dw", vslabs[-1][1]),) if kc == 0 else ())
                dst = dst_slot[:, dst_base + kc * 1024 + ot * 512:
                               dst_base + kc * 1024 + ot * 512 + 512]
                if kc % 2 == 0:
                    P.wait("scalar", "t")
                    P.op("scalar", lambda e, dst=dst, pbank=pbank:
                         e.activation(dst, pbank[:, :], AF.Copy), inc="a")
                else:
                    P.wait("vector", "t")
                    P.op("vector", lambda e, dst=dst, pbank=pbank:
                         e.tensor_copy(dst, pbank[:, :]), inc="v")
                P.wait("tensor", "a", "v")
            ws_v.release_last(8)

    def emit_layer0(self):
        P = self.P
        xn = [self.ch(self.B0, c) for c in range(8)]
        self.emit_ln(self.x_views, xn, self.B1)
        self.tap("tap_xn", xn)
        self.spill_x()

        Xb, SMb = self.Xb, self.SMb
        ws = WStream(P, self.WR[:, 0:4096], 1024, 4)
        qtmp = SMb[:, 0:512]
        rtmp1 = self.SM[:, 256:768]
        rtmp2 = self.SM[:, 768:1280]
        P.wait("sync", "v")
        P.dma("sync", self.rope_cos, self.dram["hc_rope_cos"], sem="dx")
        P.dma("sync", self.rope_sin, self.dram["hc_rope_sin"], sem="dx")
        rope_dx = P.count("dx")

        for (wname, dst, do_scale) in (("L0w_WqT", self.B1, True),
                                       ("L0w_WkT", self.B2, False)):
            def evac(oc, nt, pbank, dst=dst, do_scale=do_scale):
                c0 = oc * 2048 + nt * 512
                sl = slice(nt * 512, (nt + 1) * 512)
                P.wait("scalar", "t", "v")
                P.op("scalar", lambda e, pbank=pbank: e.activation(
                    qtmp[:], pbank[:, :], AF.Copy,
                    scale=0.125 if do_scale else 1.0), inc="a")
                P.wait("tensor", "a")
                P.op("tensor", lambda e: e.matmul(
                    self.pb[4][:, :], self.rot[:], qtmp[:],
                    start=True, stop=True), inc="t")
                P.wait("vector", "t", ("dx", rope_dx))
                P.op("vector", lambda e, sl=sl: e.tensor_tensor(
                    rtmp1[:], self.pb[4][:, :],
                    self.rope_sin[:, sl], ALU.mult))
                P.op("vector", lambda e, sl=sl: e.tensor_tensor(
                    rtmp2[:], qtmp[:], self.rope_cos[:, sl],
                    ALU.mult))
                P.op("vector", lambda e, c0=c0, dst=dst: e.tensor_tensor(
                    dst[:, c0:c0 + 512], rtmp2[:],
                    rtmp1[:], ALU.add), inc="v")
            self.project(wname, xn, evac, ws)

        self.proj_v_tok("L0w_WvT", Xb, 0)

        self.tap("tap_l0q", [self.ch(self.B1, c) for c in range(8)])
        self.tap("tap_l0k", [self.ch(self.B2, c) for c in range(8)])
        self.tap("tap_l0v", [Xb[:, c * 2048:(c + 1) * 2048] for c in range(8)])
        self.attn_core_softmax(q_slot=self.B1, k_slot=self.B2, v_tok=Xb,
                               causal=True, esc=None)
        self.tap("tap_l0ctx", [self.ch(self.B1, c) for c in range(8)])

        ctx = [self.ch(self.B1, c) for c in range(8)]
        ws2 = WStream(P, self.WR[:, 0:4096], 1024, 4)

        def evac_s(oc, nt, pbank):
            dst = self.ch(self.B2, oc, nt * 512, 512)
            P.wait("scalar", "t")
            P.op("scalar", lambda e, dst=dst, pbank=pbank: e.activation(
                dst, pbank[:, :], AF.Copy), inc="a")
        self.project("L0w_WoT", ctx, evac_s, ws2)
        P.wait("vector", "a")
        P.wait("tensor", "a")
        self.tap("tap_l0s", [self.ch(self.B2, c) for c in range(8)])

        self.l0_ffn()

    def attn_core_softmax(self, q_slot, k_slot, v_tok, causal, esc=None,
                          mode="softmax", q_base=0, k_base=0, v_base=0):
        P = self.P
        SMb = self.SMb
        P_bf = SMb[:, 0:2048]
        srow_d = self.SM[:, 1024:1536]
        pt_sb = SMb[:, 3072:3584]
        m_col = self.cols[:, 0:1]
        negm = self.cols[:, 1:2]
        l_col = self.cols[:, 2:3]
        rl_col = self.cols[:, 3:4]
        lparts = self.cols[:, 8:16]
        lparts2 = self.cols[:, 32:40]
        P.wait("tensor", "v", "a")
        nh, nqi = getattr(self, "attn_limit", (16, 16))
        for h in range(nh):
            hp, hb = h // 2, (h % 2) * 64
            for qi in range(nqi):
                nkb = qi // 4 + 1 if causal else 4
                q_ap = q_slot[hb:hb + 64, q_base + hp * 2048 + qi * 128:
                              q_base + hp * 2048 + qi * 128 + 128]
                P.wait("tensor", "a")
                for kb in range(nkb):
                    k_ap = k_slot[hb:hb + 64, k_base + hp * 2048 + kb * 512:
                                  k_base + hp * 2048 + kb * 512 + 512]
                    P.op("tensor", lambda e, kb=kb, q_ap=q_ap, k_ap=k_ap:
                         e.matmul(self.pb[kb][:, :], q_ap, k_ap,
                                  start=True, stop=True),
                         inc="t" if kb == nkb - 1 else None)
                fixed = {}
                if esc is not None:
                    fixed = esc(h, qi) or {}
                if getattr(self, "tap_iter", None) == (h, qi) and \
                        "tap_sraw" in self.debug_taps:
                    P.wait("scalar", "t")
                    P.op("scalar", lambda e: e.activation(
                        srow_d[:], self.pb[0][:, :], AF.Copy), inc="a")
                    self.tap1("tap_sraw", srow_d[:])
                P.wait("vector", "t", "a")
                if getattr(self, "tap_iter", None) == (h, qi) and \
                        "tap_sbank" in self.debug_taps:
                    P.vop(lambda e: e.tensor_copy(srow_d[:], self.pb[0][:, :]))
                    self.tap1("tap_sbank", srow_d[:])
                if mode == "softmax":
                    bm = self.cols[:, 40:48]
                    if causal:
                        P.op("vector", lambda e, nkb=nkb, qi=qi:
                             e.tensor_tensor(
                                 srow_d[:], self.pb[nkb - 1][:, :],
                                 self.cmask[:, (qi % 4) * 512:
                                            (qi % 4) * 512 + 512], ALU.add))
                        P.op("vector", lambda e, nkb=nkb: e.reduce_max(
                            bm[:, nkb - 1:nkb], srow_d[:], axis=AX.X),
                            inc="v")
                        rng = range(nkb - 1)
                    else:
                        rng = range(nkb)
                    for kb in rng:
                        srckb = fixed.get(kb, self.pb[kb])
                        P.op("vector", lambda e, kb=kb, srckb=srckb:
                             e.reduce_max(bm[:, kb:kb + 1], srckb[:, :],
                                          axis=AX.X), inc="v")
                    P.wait("vector", "v")
                    P.vop(lambda e, nkb=nkb: e.reduce_max(
                        m_col[:], bm[:, 0:nkb], axis=AX.X))
                    P.op("vector", lambda e: e.tensor_scalar(
                        negm[:], m_col[:], -1.0, None, ALU.mult, ALU.bypass),
                        inc="v")
                    P.wait("scalar", "v", "t")
                    for kb in range(nkb):
                        src = srow_d[:] if (causal and kb == nkb - 1) \
                            else fixed.get(kb, self.pb[kb])[:, :]
                        P.op("scalar", lambda e, kb=kb, src=src: e.activation(
                            P_bf[:, kb * 512:(kb + 1) * 512], src, AF.Exp,
                            bias=negm[:], accum_out=lparts[:, kb:kb + 1]))
                else:
                    P.wait("scalar", "t")
                    for kb in range(nkb):
                        P.op("scalar", lambda e, kb=kb: e.activation(
                            P_bf[:, kb * 512:(kb + 1) * 512],
                            self.pb[kb][:, :], AF.Sigmoid,
                            accum_out=lparts[:, kb:kb + 1]))
                P.op("scalar", lambda e, nkb=nkb: e.activation(
                    lparts2[:, 0:nkb], lparts[:, 0:nkb], AF.Copy), inc="a")
                P.wait("vector", "a")
                eps_l = 1e-6 if mode == "sigmoid" else 0.0
                P.vop(lambda e, nkb=nkb, eps_l=eps_l: e.tensor_scalar(
                    l_col[:], lparts2[:, 0:nkb], eps_l, None, ALU.add,
                    ALU.bypass, accum_out=None) if False else e.reduce_sum(
                    l_col[:], lparts2[:, 0:nkb], axis=AX.X))
                if mode == "sigmoid":
                    P.vop(lambda e: e.tensor_scalar(
                        l_col[:], l_col[:], 1e-6, None, ALU.add, ALU.bypass))
                P.vred(lambda e: e.reciprocal(rl_col[:], l_col[:]))
                P.op("vector", lambda e, nkb=nkb: e.tensor_scalar(
                    P_bf[:, 0:nkb * 512], P_bf[:, 0:nkb * 512], rl_col[:],
                    None, ALU.mult, ALU.bypass), inc="v")
                if getattr(self, "tap_iter", None) == (h, qi):
                    self.tap1("tap_pbf", P_bf[:, 0:2048])
                    self.tap1("tap_cols", self.cols[:, 0:64])
                    self.tap1("tap_srowd", srow_d[:])
                nkc = qi + 1 if causal else 16
                ctx_ps = self.pb[6][0:64, 0:128]
                first_av = True
                for g0 in range(0, nkc, 4):
                    gn = min(4, nkc - g0)
                    bank = 4 + (g0 // 4) % 2
                    P.wait("tensor", "v")
                    for j in range(gn):
                        kc = g0 + j
                        P.op("tensor", lambda e, j=j, kc=kc, bank=bank:
                             e.transpose(
                                 self.pbb[bank][:, j * 128:(j + 1) * 128],
                                 P_bf[:, kc * 128:(kc + 1) * 128],
                                 self.identb[:]),
                             inc="t" if j == gn - 1 else None)
                    P.wait("vector", "t")
                    P.op("vector", lambda e, gn=gn, bank=bank: e.tensor_copy(
                        pt_sb[:, 0:gn * 128], self.pbb[bank][:, 0:gn * 128]),
                        inc="v")
                    P.wait("tensor", "v")
                    for j in range(gn):
                        kc = g0 + j
                        v_ap = v_tok[:, v_base + kc * 1024 + h * 64:
                                     v_base + kc * 1024 + h * 64 + 64]
                        P.op("tensor", lambda e, j=j, v_ap=v_ap,
                             st=(g0 + j == nkc - 1), s0=first_av: e.matmul(
                                 ctx_ps, v_ap,
                                 pt_sb[:, j * 128:(j + 1) * 128],
                                 start=s0, stop=st, skip_group_check=True),
                             inc="t" if j == gn - 1 else None)
                        first_av = False
                P.wait("scalar", "t")
                dst = q_slot[hb:hb + 64, q_base + hp * 2048 + qi * 128:
                             q_base + hp * 2048 + qi * 128 + 128]
                P.op("scalar", lambda e, dst=dst: e.activation(
                    dst, ctx_ps, AF.Copy), inc="a")

    def l0_ffn(self):
        P = self.P
        SPb = self.SPb
        I, NIC = 2816, 22
        gwr = WStream(P, self.smallw[:, 0:64], 64, 1)
        gw, gwn = gwr.load(w_slab(self.dram["L0_ffn_gate_w"], 0, 8, 0, 8),
                           8, 8)

        def logits(tt, pbank):
            tiles = [(self.ch(self.B0, c, tt * 128, 128),
                      gw[:, c * 8:(c + 1) * 8]) for c in range(8)]
            mm_acc(P, pbank[:, 0:8], tiles,
                   waits=(("dw", gwn),) if tt == 0 else ())
        CWb = self.gate_cwf(logits, thr=None)
        self.tap1("tap_l0cw", CWb[:, 0:128])

        wgu_all = [self.dram["L0_ffn_base_wgu"]] + \
            [self.dram[f"L0_ffn_exp_wgu_{i}"] for i in range(8)]
        wd_all = [self.dram["L0_ffn_base_wd"]] + \
            [self.dram[f"L0_ffn_exp_wd_{i}"] for i in range(8)]
        BW = self.SP[:, 3072:4096]
        t1 = self.SM[:, 1024:1536]

        def a_tile(ic, t0, tn):
            if ic < 16:
                return self.B1[:, ic * 1024 + t0: ic * 1024 + t0 + tn]
            return SPb[:, (ic - 16) * 1024 + t0: (ic - 16) * 1024 + t0 + tn]

        P.wait("vector", ("dx", self.x_spilled_at))
        nexp = getattr(self, "l0_nexp", 9)
        for tb in range(2):
            for ei in range(nexp):
                if ei > 0:
                    self.bw_from_cw(CWb, ei - 1, tb * 8, 8, BW)
                    if ei == 1 and tb == 0:
                        self.tap1("tap_l0bw", BW[:, 0:1024])
                ws_gu = WStream(P, self.WR[:, 0:4096], 1024, 4)
                for ic in range(NIC):
                    gslab, _ = ws_gu.load(
                        w_slab(wgu_all[ei], 0, 8, ic * 128, 128), 8, 128)
                    uslab, gn2 = ws_gu.load(
                        w_slab(wgu_all[ei], 0, 8, I + ic * 128, 128), 8, 128)
                    for nt2 in range(2):
                        pg = self.pb[0 + nt2]
                        pu = self.pb[2 + nt2]
                        tg = [(gslab[:, j * 128:(j + 1) * 128],
                               self.ch(self.B0, j, tb * 1024 + nt2 * 512, 512))
                              for j in range(8)]
                        tu = [(uslab[:, j * 128:(j + 1) * 128],
                               self.ch(self.B0, j, tb * 1024 + nt2 * 512, 512))
                              for j in range(8)]
                        mm_acc(P, pg[:, :], tg,
                               waits=(("dw", gn2),) if nt2 == 0 else ())
                        mm_acc(P, pu[:, :], tu)
                        P.wait("scalar", "t", "v")
                        P.op("scalar", lambda e, pg=pg: e.activation(
                            t1[:], pg[:, :], AF.Silu), inc="a")
                        P.wait("vector", "a")
                        P.op("vector", lambda e, pu=pu, ic=ic, nt2=nt2:
                             e.tensor_tensor(a_tile(ic, nt2 * 512, 512),
                                             t1[:], pu[:, :], ALU.mult),
                             inc="v")
                        P.wait("tensor", "v")
                    ws_gu.release_last(2)
                ws_d = WStream(P, self.WR[:, 0:5632], 2816, 2)
                for oc in range(8):
                    dslab, dn = ws_d.load(
                        w_slab(wd_all[ei], 0, NIC, oc * 128, 128), NIC, 128)
                    for nt2 in range(2):
                        pd = self.pb[4 + nt2]
                        tiles = [(dslab[:, j * 128:(j + 1) * 128],
                                  a_tile(j, nt2 * 512, 512))
                                 for j in range(NIC)]
                        mm_acc(P, pd[:, :], tiles,
                               waits=(("dw", dn),) if nt2 == 0 else ())
                        P.wait("vector", "t")
                        acc = self.acc_ap(oc, tb, nt2 * 512, 512)
                        if ei == 0:
                            P.op("vector", lambda e, acc=acc, pd=pd:
                                 e.tensor_copy(acc, pd[:, :]), inc="v")
                        else:
                            P.op("vector", lambda e, pd=pd, nt2=nt2:
                                 e.tensor_tensor(
                                     t1[:], pd[:, :],
                                     BW[:, nt2 * 512:(nt2 + 1) * 512],
                                     ALU.mult))
                            P.op("vector", lambda e, acc=acc: e.tensor_tensor(
                                acc, acc, t1[:], ALU.add), inc="v")
                        P.wait("tensor", "v")
                    ws_d.release_last()
            if f"tap_l0acc{tb}" in self.debug_taps:
                self.tap(f"tap_l0acc{tb}",
                         [self.acc_ap(oc, tb) for oc in range(8)], cols=1024)
            if not getattr(self, "skip_combine", False):
                if tb == 0:
                    x3 = self.X[:].rearrange("p (c t) -> p c t", t=2048)
                    dst3 = x3[:, :, 1024:2048]
                    views = [self.xv(oc, 1024, 1024) for oc in range(8)]
                    self.combine_tb(tb, s_slot=self.B2, xold_f32_region=None,
                                    xold_views=(dst3, views))
                else:
                    self.combine_tb(tb, s_slot=self.B2,
                                    xold_f32_region=self.B1f[:, 0:8192])

    def emit_layer1(self):
        P = self.P
        SMb, SM, Xb, B2 = self.SMb, self.SM, self.Xb, self.B2
        col = lambda name, j: self.bias[:, BCMAP[name] + j:
                                        BCMAP[name] + j + 1]
        xn = [self.ch(self.B0, c) for c in range(8)]
        self.emit_ln(self.x_views, xn, self.B1)
        self.spill_x()

        acos = self.Xb[:, 2048:4096]
        asin = self.Xb[:, 4096:6144]
        arow = SM[:, 1024:1536]
        prow = SM[:, 1536:2048]
        srow2 = self.cols[:, 40:48]
        P.wait("scalar", ("dx", self.x_spilled_at))
        P.wait("vector", ("dx", self.x_spilled_at))
        ws0 = WStream(P, self.smallw[:, 128:384], 128, 2)
        ampw, _ = ws0.load(w_slab(self.dram["L1w_ampT"], 0, 8, 0, 16), 8, 16)
        phw, wn0 = ws0.load(w_slab(self.dram["L1w_phT"], 0, 8, 0, 16), 8, 16)
        for nt in range(4):
            P.wait("tensor", "a", "v")
            tiles = [(ampw[:, c * 16:(c + 1) * 16],
                      xn[c][:, nt * 512:(nt + 1) * 512]) for c in range(8)]
            mm_acc(P, self.pb[6][0:16, :], tiles,
                   waits=(("dw", wn0),) if nt == 0 else ())
            tiles = [(phw[:, c * 16:(c + 1) * 16],
                      xn[c][:, nt * 512:(nt + 1) * 512]) for c in range(8)]
            mm_acc(P, self.pb[7][0:16, :], tiles)
            P.wait("scalar", "t")
            P.aop(lambda e, nt=nt: e.activation(
                arow[0:16, :], self.pb[6][0:16, :], AF.Sigmoid,
                bias=self.bias[0:16, BCMAP["l1_ampb"]:BCMAP["l1_ampb"] + 1]))
            P.aop(lambda e, nt=nt: e.activation(
                prow[0:16, :], self.pb[7][0:16, :], AF.Tanh,
                bias=self.bias[0:16, BCMAP["l1_phb"]:BCMAP["l1_phb"] + 1]))
            P.aop(lambda e, nt=nt: e.activation(
                acos[0:16, nt * 512:(nt + 1) * 512], prow[0:16, :], AF.Sin,
                scale=math.pi / 2.0))
            P.aop(lambda e, nt=nt: e.activation(
                asin[0:16, nt * 512:(nt + 1) * 512], prow[0:16, :], AF.Sin,
                scale=math.pi))
            P.wait("vector", "a")
            P.vop(lambda e, nt=nt: e.tensor_tensor(
                prow[0:16, :], acos[0:16, nt * 512:(nt + 1) * 512],
                acos[0:16, nt * 512:(nt + 1) * 512], ALU.mult))
            P.vop(lambda e, nt=nt: e.tensor_scalar(
                prow[0:16, :], prow[0:16, :], -2.0, 1.0, ALU.mult, ALU.add))
            P.vop(lambda e, nt=nt: e.scalar_tensor_tensor(
                prow[0:16, :], prow[0:16, :], 0.3, arow[0:16, :], ALU.mult,
                ALU.mult))
            P.vop(lambda e, nt=nt: e.tensor_copy(
                acos[0:16, nt * 512:(nt + 1) * 512], prow[0:16, :]))
            P.vop(lambda e, nt=nt: e.scalar_tensor_tensor(
                asin[0:16, nt * 512:(nt + 1) * 512],
                asin[0:16, nt * 512:(nt + 1) * 512], 0.3, arow[0:16, :],
                ALU.mult, ALU.mult))
            P.wait("tensor", "v")
        for h in range(16):
            P.vop(lambda e, h=h: e.tensor_scalar(
                self.cmask[0:16, h * 64:(h + 1) * 64],
                self.iota_p[0:16, 0:64], float(h), None, ALU.is_equal,
                ALU.bypass))

        self.tap1("tap_l1acos", acos[0:16, 0:2048])
        self.tap1("tap_l1asin", asin[0:16, 0:2048])
        ws = WStream(P, self.WR[:, 0:4096], 1024, 4)

        def evac_v(oc, nt, pbank):
            dst = self.ch(self.B1, oc, nt * 512, 512)
            P.wait("scalar", "t")
            P.op("scalar", lambda e, dst=dst, pbank=pbank, oc=oc:
                 e.activation(dst, pbank[:, :], AF.Identity,
                              bias=col("l1_bv", oc)), inc="a")
        self.project("L1w_WvT", xn, evac_v, ws)

        self.tap("tap_l1v", [self.ch(self.B1, c) for c in range(8)])
        P.wait("vector", ("dx", self.x_spilled_at))
        for hp in range(8):
            P.vop(lambda e, hp=hp: e.tensor_scalar(
                self.cmask[0:16, hp * 128:hp * 128 + 64],
                self.iota_p[0:16, 0:64], float(2 * hp), None, ALU.is_equal,
                ALU.bypass))
            P.vop(lambda e, hp=hp: e.tensor_scalar(
                self.cmask[0:16, hp * 128 + 64:hp * 128 + 128],
                self.iota_p[0:16, 0:64], float(2 * hp + 1), None,
                ALU.is_equal, ALU.bypass))
        mixslot = [self.WR[:, 4096 - 512 + i * 128:4096 - 512 + (i + 1) * 128]
                   for i in range(4)]
        mring = {"n": 0, "guard": [("t", P.count("t"))] * 4}

        def load_mix_bd(wname, hp):
            i = mring["n"] % 4
            mring["n"] += 1
            slot = mixslot[i]
            P.wait("gpsimd", mring["guard"][i])
            P.op("gpsimd", lambda e, slot=slot: e.memset(
                slot[0:64, 64:128], 0.0))
            P.op("gpsimd", lambda e, slot=slot: e.memset(
                slot[64:128, 0:64], 0.0))
            P.dma("gpsimd", slot[0:64, 0:64],
                  self.dram[wname][2 * hp * 64:(2 * hp + 1) * 64, :],
                  sem="dw")
            P.dma("gpsimd", slot[64:128, 64:128],
                  self.dram[wname][(2 * hp + 1) * 64:(2 * hp + 2) * 64, :],
                  sem="dw")
            return slot, P.count("dw"), i
        for hp in range(8):
            s_m, wn1, i1 = load_mix_bd("L1w_mixT", hp)
            s_r, wn2, i2 = load_mix_bd("L1w_mixrT", hp)
            sel = self.cmask[0:16, hp * 128:(hp + 1) * 128]
            for nt in range(4):
                vt = self.B1[:, hp * 2048 + nt * 512:
                             hp * 2048 + nt * 512 + 512]
                P.wait("tensor", ("dw", wn2), ("p", P.count("p")), "v", "a")
                P.op("tensor", lambda e, s_m=s_m, vt=vt: e.matmul(
                    self.pb[0][:, :], s_m, vt, start=True, stop=True))
                P.op("tensor", lambda e, s_r=s_r, vt=vt: e.matmul(
                    self.pb[1][:, :], s_r, vt, start=True, stop=True))
                P.op("tensor", lambda e, sel=sel, nt=nt: e.matmul(
                    self.pb[2][:, :], sel,
                    acos[0:16, nt * 512:(nt + 1) * 512],
                    start=True, stop=True))
                P.op("tensor", lambda e, sel=sel, nt=nt: e.matmul(
                    self.pb[3][:, :], sel,
                    asin[0:16, nt * 512:(nt + 1) * 512],
                    start=True, stop=True), inc="t")
                P.wait("scalar", "t")
                sup_sb = Xb[:, 0:512]
                supr_sb = Xb[:, 512:1024]
                P.aop(lambda e, hp=hp, sup_sb=sup_sb: e.activation(
                    sup_sb, self.pb[0][:, :], AF.Identity,
                    bias=self.bias[:, BCMAP["l1_mixb"] + hp:
                                   BCMAP["l1_mixb"] + hp + 1]))
                P.aop(lambda e, hp=hp, supr_sb=supr_sb: e.activation(
                    supr_sb, self.pb[1][:, :], AF.Identity,
                    bias=self.bias[:, BCMAP["l1_mixbr"] + hp:
                                   BCMAP["l1_mixbr"] + hp + 1]))
                P.wait("vector", "a")
                mfx = Xb[:, 1024:1536]
                P.vop(lambda e, sup_sb=sup_sb, mfx=mfx: e.tensor_tensor(
                    mfx, self.pb[2][:, :], sup_sb, ALU.mult))
                P.vop(lambda e, vt=vt, mfx=mfx: e.tensor_tensor(
                    mfx, mfx, vt, ALU.add))
                P.vop(lambda e, supr_sb=supr_sb: e.tensor_tensor(
                    Xb[:, 1536:2048], self.pb[3][:, :], supr_sb, ALU.mult))
                P.vop(lambda e, mfx=mfx: e.tensor_tensor(
                    mfx, mfx, Xb[:, 1536:2048], ALU.add))
                P.wait("tensor", "v")
                for jj in range(4):
                    P.op("tensor", lambda e, jj=jj, mfx=mfx: e.transpose(
                        self.pbb[4][:, jj * 128:(jj + 1) * 128],
                        mfx[:, jj * 128:(jj + 1) * 128], self.identb[:]),
                        inc="t" if jj == 3 else None)
                P.wait("vector", "t")
                for jj in range(4):
                    kc = nt * 4 + jj
                    P.op("vector", lambda e, jj=jj, kc=kc, hp=hp:
                         e.tensor_copy(
                             B2[:, kc * 1024 + hp * 128:
                                kc * 1024 + hp * 128 + 128],
                             self.pbb[4][:, jj * 128:(jj + 1) * 128]),
                         inc="v" if jj == 3 else None)
                P.wait("tensor", "v")
            mring["guard"][i1] = ("t", P.count("t"))
            mring["guard"][i2] = ("t", P.count("t"))

        ws1 = WStream(P, self.WR[:, 0:4096], 1024, 4)

        def mk_evac(dst_slot, base, bias_name, scale):
            def evac(oc, nt, pbank):
                dst = dst_slot[:, base + oc * 2048 + nt * 512:
                               base + oc * 2048 + nt * 512 + 512]
                P.wait("scalar", "t")
                P.op("scalar", lambda e, dst=dst, pbank=pbank, oc=oc:
                     e.activation(dst, pbank[:, :], AF.Identity,
                                  bias=col(bias_name, oc), scale=scale),
                     inc="a")
            return evac
        P.vop(lambda e: e.tensor_scalar(
            self.bias[:, BCMAP["l1_bqs"]:BCMAP["l1_bqs"] + 8],
            self.bias[:, BCMAP["l1_bq"]:BCMAP["l1_bq"] + 8], 0.125, None,
            ALU.mult, ALU.bypass))
        self.project("L1w_WkT", xn, mk_evac(self.B1, 0, "l1_bk", 1.0), ws1)
        self.project("L1w_WekT", xn, mk_evac(Xb, 0, "l1_bek", 1.0), ws1)
        self.project("L1w_WqT", xn, mk_evac(Xb, 16384, "l1_bqs", 0.125), ws1)

        eqws = WStream(P, self.WR[:, 0:1024], 512, 2)
        eq_state = {}

        def esc_cb(h, qi):
            hp, hb = h // 2, (h % 2) * 64
            if getattr(self, "tap_iter", None) == (h, qi) and \
                    "tap_spre" in self.debug_taps:
                P.wait("vector", "t")
                P.vop(lambda e: e.tensor_copy(SM[:, 1024:1536],
                                              self.pb[0][:, :]))
                self.tap1("tap_spre", SM[:, 1024:1536])
            if eq_state.get("h") != h:
                eqws.release_last()
                slab, wn = eqws.load(
                    w_slab(self.dram["L1w_WeqT"], 0, 8, h * 64, 64), 8, 64)
                eq_state.update(h=h, slab=slab, wn=wn)
            slab = eq_state["slab"]
            P.wait("tensor", "a", "v")
            eqt = SMb[hb:hb + 64, 3584:3712]
            tiles = [(slab[:, c * 64:(c + 1) * 64],
                      self.ch(self.B0, c, qi * 128, 128)) for c in range(8)]
            for jj, (l, r) in enumerate(tiles):
                P.op("tensor", lambda e, l=l, r=r, s=(jj == 0),
                     st=(jj == 7), hb=hb: e.matmul(
                         self.pb[5][hb:hb + 64, 0:128] if hb else
                         self.pb[5][0:64, 0:128], l, r, start=s, stop=st,
                         tile_position=(0, hb), skip_group_check=True),
                     inc="t" if jj == 7 else None,
                     waits=(("dw", eq_state["wn"]),) if jj == 0 else ())
            P.wait("scalar", "t")
            P.aop(lambda e, hb=hb, eqt=eqt: e.activation(
                eqt, self.pb[5][hb:hb + 64, 0:128] if hb else
                self.pb[5][0:64, 0:128], AF.Identity,
                bias=col("l1_beq", h // 2) if False else
                self.beq_col(h, hb)))
            if getattr(self, "tap_iter", None) == (h, qi):
                self.tap1("tap_eqt", SMb[0:64, 3584:3712])
            kjs = list(range(max(0, qi - 2), min(16, qi + 3)))
            banks = sorted({kj // 4 for kj in kjs})
            fixed = {}
            P.wait("vector", "t")
            for bi, kb in enumerate(banks):
                srow = self.SP[:, bi * 512:(bi + 1) * 512]
                fixed[kb] = srow
                P.vop(lambda e, srow=srow, kb=kb: e.tensor_copy(
                    srow, self.pb[kb][:, :]))
            for kj in kjs:
                ek_t = Xb[hb:hb + 64, hp * 2048 + kj * 128:
                          hp * 2048 + kj * 128 + 128]
                P.wait("tensor", "v")
                P.op("tensor", lambda e, eqt=eqt, ek_t=ek_t: e.matmul(
                    self.pb[4][:, 0:128], eqt, ek_t, start=True, stop=True,
                    skip_group_check=True), inc="t")
                P.wait("vector", "t")
                dk = (qi - kj) + 2
                scol = fixed[kj // 4][:, (kj % 4) * 128:(kj % 4) * 128 + 128]
                P.vop(lambda e, dk=dk: e.scalar_tensor_tensor(
                    SM[:, 1664:1792], self.pb[4][:, 0:128], 0.0625,
                    self.decay[:, dk * 128:(dk + 1) * 128], ALU.mult,
                    ALU.mult))
                P.vop(lambda e, scol=scol: e.tensor_tensor(
                    scol, scol, SM[:, 1664:1792], ALU.add))
            return fixed
        self.tap("tap_l1q", [Xb[:, 16384 + c * 2048:16384 + (c + 1) * 2048]
                             for c in range(8)])
        self.tap("tap_l1k", [self.ch(self.B1, c) for c in range(8)])
        self.tap("tap_l1ek", [Xb[:, c * 2048:(c + 1) * 2048]
                              for c in range(8)])
        self.tap("tap_l1mv", [B2[:, c * 2048:(c + 1) * 2048]
                              for c in range(8)])
        self.attn_core_softmax(q_slot=Xb, k_slot=self.B1, v_tok=B2,
                               causal=False, esc=esc_cb, mode="softmax",
                               q_base=16384)
        self.tap("tap_l1ctx", [Xb[:, 16384 + c * 2048:16384 + (c + 1) * 2048]
                               for c in range(8)])

        ctx = [Xb[:, 16384 + c * 2048:16384 + (c + 1) * 2048]
               for c in range(8)]
        ws2 = WStream(P, self.WR[:, 0:4096], 1024, 4)

        def res_view(c, t0=0, tn=2048):
            if c < 4:
                return self.B1f[:, c * 2048 + t0:c * 2048 + t0 + tn]
            return self.B2f[:, (c - 4) * 2048 + t0:(c - 4) * 2048 + t0 + tn]

        def evac_res(oc, nt, pbank):
            P.wait("scalar", "t")
            P.op("scalar", lambda e, oc=oc, nt=nt, pbank=pbank: e.activation(
                SM[:, 1024:1536], pbank[:, :], AF.Identity,
                bias=col("l1_bo", oc)), inc="a")
            P.wait("vector", "a")
            P.op("vector", lambda e, oc=oc, nt=nt: e.tensor_tensor(
                res_view(oc, nt * 512, 512), SM[:, 1024:1536],
                self.ch(self.B0, oc, nt * 512, 512), ALU.add), inc="v")
            P.wait("scalar", "v")
        self.project("L1w_WoT", ctx, evac_res, ws2)
        P.wait("tensor", "v")
        res_views = [res_view(c) for c in range(8)]
        s_views = [Xb[:, c * 2048:(c + 1) * 2048] for c in range(8)]
        self.tap("tap_l1res", res_views)
        self.emit_ln(res_views, s_views, Xb[:, 16384:32768],
                     gamma_col=self.bias[:, BCMAP["l1_lng"]:
                                         BCMAP["l1_lng"] + 8],
                     beta_col=self.bias[:, BCMAP["l1_lnb"]:
                                        BCMAP["l1_lnb"] + 8])
        self.tap("tap_l1s", [Xb[:, c * 2048:(c + 1) * 2048] for c in range(8)])
        self.l1_ffn()

    def mixb_col(self, h, hb):
        c = BCMAP["l1_mixb"] + h // 2
        return self.bias[hb:hb + 64, c:c + 1] if hb else \
            self.bias[0:64, c:c + 1]

    def mixbr_col(self, h, hb):
        c = BCMAP["l1_mixbr"] + h // 2
        return self.bias[hb:hb + 64, c:c + 1] if hb else \
            self.bias[0:64, c:c + 1]

    def beq_col(self, h, hb):
        c = BCMAP["l1_beq"] + h // 2
        return self.bias[hb:hb + 64, c:c + 1] if hb else \
            self.bias[0:64, c:c + 1]

    def l1_ffn(self):
        P = self.P
        SMb, SM, Xb = self.SMb, self.SM, self.Xb
        col = lambda name, j: self.bias[:, BCMAP[name] + j:
                                        BCMAP[name] + j + 1]
        xn = [self.ch(self.B0, c) for c in range(8)]
        xnsq = Xb[:, 16384:32768]
        P.wait("scalar", "v")
        for c in range(8):
            P.op("scalar", lambda e, c=c: e.activation(
                xnsq[:, c * 2048:(c + 1) * 2048], xn[c], AF.Square),
                inc="a" if c == 7 else None)
        vgws = WStream(P, self.WR[:, 0:2048], 512, 4)
        xet = SMb[:, 0:2048]
        xesq = SMb[:, 2048:4096]
        Msb = self.smallw[0:64, 0:64]
        P.wait("tensor", "a")
        for ei in range(8):
            slab, wn = vgws.load(
                w_slab(self.dram[f"L1f_Vg_{ei}"], 0, 8, 0, 64), 8, 64)
            tiles = [(slab[:, c * 64:(c + 1) * 64],
                      slab[:, c * 64:(c + 1) * 64]) for c in range(8)]
            mm_acc(P, self.pb[6][0:64, 0:64], tiles,
                   waits=(("dw", wn), "v"))
            P.wait("vector", "t")
            P.vop(lambda e: e.tensor_copy(Msb, self.pb[6][0:64, 0:64]))
            for nt in range(4):
                tiles = [(slab[:, c * 64:(c + 1) * 64],
                          xn[c][:, nt * 512:(nt + 1) * 512])
                         for c in range(8)]
                mm_acc(P, self.pb[7][0:64, :], tiles)
                P.wait("scalar", "t")
                P.aop(lambda e, nt=nt: e.activation(
                    xet[0:64, nt * 512:(nt + 1) * 512],
                    self.pb[7][0:64, :], AF.Copy))
                P.aop(lambda e, nt=nt: e.activation(
                    xesq[0:64, nt * 512:(nt + 1) * 512],
                    self.pb[7][0:64, :], AF.Square))
                P.wait("tensor", "a")
            vgws.release_last()
            for nt in range(4):
                pr = self.pb[nt][0:8, :]
                first = (ei == 0)
                tiles = []
                if first:
                    tiles += [(self.onesb_col[:, 0:8],
                               xnsq[:, c * 2048 + nt * 512:
                                    c * 2048 + nt * 512 + 512])
                              for c in range(8)]
                if nt == 0:
                    P.wait("tensor", "v")
                P.op("tensor", lambda e, nt=nt: e.matmul(
                    self.pb[5][0:64, :], Msb,
                    xet[0:64, nt * 512:(nt + 1) * 512], start=True,
                    stop=True, skip_group_check=True), inc="t")
                P.wait("vector", "t")
                P.vop(lambda e, nt=nt: e.tensor_tensor(
                    self.B2[0:64, 0:512],
                    xet[0:64, nt * 512:(nt + 1) * 512],
                    self.pb[5][0:64, :], ALU.mult))
                P.wait("tensor", "v")
                segs = tiles + [
                    (self.sel8m2[0:64, ei * 8:(ei + 1) * 8],
                     xesq[0:64, nt * 512:(nt + 1) * 512]),
                    (self.sel8p1[0:64, ei * 8:(ei + 1) * 8],
                     self.B2[0:64, 0:512]),
                ]
                n = len(segs)
                for jj, (l, r) in enumerate(segs):
                    P.op("tensor", lambda e, l=l, r=r, pr=pr,
                         s=(first and jj == 0),
                         st=(ei == 7 and jj == n - 1): e.matmul(
                             pr, l, r, start=s, stop=st,
                             skip_group_check=True),
                         inc="t" if jj == n - 1 else None)
        Rrows = SM[0:8, 0:2048]
        P.wait("vector", "t")
        for nt in range(4):
            P.vop(lambda e, nt=nt: e.tensor_scalar(
                Rrows[:, nt * 512:(nt + 1) * 512], self.pb[nt][0:8, :],
                0.0, None, ALU.max, ALU.bypass))
        P.wait("scalar", "v")
        P.aop(lambda e: e.activation(Rrows[:], Rrows[:], AF.Sqrt))
        self.tap1("tap_l1R", Rrows[:])
        Rb = self.B2[0:8, 512:640]

        def logits(tt, pbank):
            P.wait("vector", "a", "t")
            P.wait("tensor", "v", "a")
            P.op("tensor", lambda e, tt=tt: e.transpose(
                self.pb[6][:, 0:8], Rrows[:, tt * 128:(tt + 1) * 128],
                self.identf[0:8, 0:8]), inc="t")
            P.wait("vector", "t")
            P.vop(lambda e: e.tensor_scalar(
                SM[:, 1664:1672], self.pb[6][:, 0:8], -1.0, None, ALU.mult,
                ALU.bypass))
            P.vop(lambda e, pbank=pbank: e.tensor_copy(
                pbank[:, 0:8], SM[:, 1664:1672]))
            P.wait("tensor", "v")
            P.op("tensor", lambda e: e.nop(), inc="t")
        CWb = self.gate_cwf(logits, thr=0.125)

        BW = self.SP[:, 0:1024]
        t1 = SM[:, 1024:1536]
        xr = self.xspill.rearrange("(c p) t -> p c t", p=128)
        for tb in range(2):
            accv = lambda oc, t0, tn: self.B1f[:, oc * 1024 + t0:
                                               oc * 1024 + t0 + tn]
            ws = WStream(P, self.WR[:, 0:4096], 1024, 4)
            wp = self.dram["L1w_WpT"]
            for oc in range(8):
                slab, wn = ws.load(w_slab(wp, 0, 8, oc * 128, 128), 8, 128)
                for nt2 in range(2):
                    pbank = self.pb[nt2]
                    tiles = [(slab[:, j * 128:(j + 1) * 128],
                              self.ch(self.B0, j, tb * 1024 + nt2 * 512, 512))
                             for j in range(8)]
                    mm_acc(P, pbank[:, :], tiles,
                           waits=(("dw", wn),) if nt2 == 0 else ())
                    P.wait("scalar", "t", "v")
                    P.op("scalar", lambda e, oc=oc, nt2=nt2, pbank=pbank:
                         e.activation(accv(oc, nt2 * 512, 512), pbank[:, :],
                                      AF.Identity, bias=col("l1_bp", oc)),
                         inc="a")
                    P.wait("tensor", "a")
                ws.release_last()
            for ei in range(8):
                self.bw_from_cw(CWb, ei, tb * 8, 8, BW)
                we = self.dram[f"L1f_WeT_{ei}"]
                for oc in range(8):
                    slab, wn = ws.load(w_slab(we, 0, 8, oc * 128, 128),
                                       8, 128)
                    for nt2 in range(2):
                        pbank = self.pb[2 + nt2]
                        tiles = [(slab[:, j * 128:(j + 1) * 128],
                                  self.ch(self.B0, j,
                                          tb * 1024 + nt2 * 512, 512))
                                 for j in range(8)]
                        mm_acc(P, pbank[:, :], tiles,
                               waits=(("dw", wn),) if nt2 == 0 else ())
                        P.wait("scalar", "t", "v")
                        P.op("scalar", lambda e, oc=oc, pbank=pbank, ei=ei:
                             e.activation(t1[:], pbank[:, :], AF.Identity,
                                          bias=col("l1_be", ei * 8 + oc)),
                             inc="a")
                        P.wait("vector", "a")
                        P.op("vector", lambda e, nt2=nt2: e.tensor_tensor(
                            t1[:], t1[:],
                            BW[:, nt2 * 512:(nt2 + 1) * 512], ALU.mult))
                        P.op("vector", lambda e, oc=oc, nt2=nt2:
                             e.tensor_tensor(accv(oc, nt2 * 512, 512),
                                             accv(oc, nt2 * 512, 512),
                                             t1[:], ALU.add), inc="v")
                        P.wait("tensor", "v")
                    ws.release_last()
            if tb == 0:
                self.tap1("tap_l1cw", CWb[:, 0:128])
                self.tap1("tap_l1acc", self.B1f[:, 0:8192])
            tsl = slice(tb * 1024, (tb + 1) * 1024)
            P.wait("sync", "v", "a", "t", ("dx", self.x_spilled_at))
            P.dma("sync",
                  self.B2f[:, 0:8192].rearrange("p (c t) -> p c t", t=1024),
                  xr[:, :, tsl], sem="dx")
            xold_dx = P.count("dx")
            gtmp = SM[:, 0:1024]
            P.wait("scalar", "v", ("dx", xold_dx))
            P.wait("vector", ("dx", xold_dx))
            for oc in range(8):
                P.op("scalar", lambda e, oc=oc: e.activation(
                    gtmp[:], accv(oc, 0, 1024), AF.Silu), inc="a")
                P.wait("vector", "a")
                s_ap = self.Xb[:, oc * 2048 + tb * 1024:
                               oc * 2048 + tb * 1024 + 1024]
                P.op("vector", lambda e, s_ap=s_ap: e.tensor_tensor(
                    gtmp[:], s_ap, gtmp[:], ALU.mult))
                xo = self.B2f[:, oc * 1024:(oc + 1) * 1024]
                P.op("vector", lambda e, xo=xo: e.tensor_tensor(
                    xo, xo, gtmp[:], ALU.add), inc="v")
                P.wait("scalar", "v")
            P.wait("sync", "v")
            P.dma("sync", xr[:, :, tsl],
                  self.B2f[:, 0:8192].rearrange("p (c t) -> p c t", t=1024),
                  sem="dx")
        P.wait("sync", ("dx", P.count("dx")))
        P.dma("sync", self.X[:].rearrange("p (c t) -> p c t", t=2048),
              xr, sem="dx")
        for eng in ("scalar", "vector", "tensor"):
            P.wait(eng, ("dx", P.count("dx")))

    def emit_layer2(self):
        P = self.P
        xn = [self.ch(self.B0, c) for c in range(8)]
        self.emit_ln(self.x_views, xn, self.B1)
        self.spill_x()
        Xb = self.Xb
        ws = WStream(P, self.WR[:, 0:4096], 1024, 4)

        for (wname, dst, do_scale) in (("L2w_WqT", self.B1, True),
                                       ("L2w_WkT", self.B2, False)):
            def evac(oc, nt, pbank, dst=dst, do_scale=do_scale):
                c0 = oc * 2048 + nt * 512
                P.wait("scalar", "t")
                P.op("scalar", lambda e, pbank=pbank, c0=c0, dst=dst:
                     e.activation(dst[:, c0:c0 + 512], pbank[:, :], AF.Copy,
                                  scale=0.125 if do_scale else 1.0), inc="a")
            self.project(wname, xn, evac, ws)
        self.proj_v_tok("L2w_WvT", Xb, 0)

        self.attn_core_softmax(q_slot=self.B1, k_slot=self.B2, v_tok=Xb,
                               causal=False, mode="sigmoid")

        ctx = [self.ch(self.B1, c) for c in range(8)]
        ws2 = WStream(P, self.WR[:, 0:4096], 1024, 4)

        def evac_s(oc, nt, pbank):
            dst = self.ch(self.B2, oc, nt * 512, 512)
            P.wait("scalar", "t")
            P.op("scalar", lambda e, dst=dst, pbank=pbank: e.activation(
                dst, pbank[:, :], AF.Copy), inc="a")
        self.project("L2w_WoT", ctx, evac_s, ws2)
        P.wait("sync", "a")
        s3d = self.sspill.rearrange("(c p) t -> p c t", p=128)
        P.dma("sync", s3d, self.B2[:].rearrange("p (c t) -> p c t", t=2048),
              sem="dx")
        self.s_spilled_at = P.count("dx")
        self.l2_ffn()

    def l2_ffn(self):
        P = self.P
        SMb = self.SMb
        BC = self.bias
        t_sil = SMb[:, 0:512]
        t2 = self.SM[:, 512:1024]
        gtmp = self.SM[:, 1024:1536]
        xr = self.xspill.rearrange("(c p) t -> p c t", p=128)
        s3d = self.sspill.rearrange("(c p) t -> p c t", p=128)
        col = lambda name, j: BC[:, BCMAP[name] + j: BCMAP[name] + j + 1]

        for tb in range(4):
            tsl = slice(tb * 512, (tb + 1) * 512)
            P.wait("tensor", "a", "v")
            ws = WStream(P, self.WR[:, 0:4096], 1024, 4)
            for ic in range(32):
                gs, _ = ws.load(w_slab(self.dram["L2f_w1T"], 0, 8,
                                       ic * 128, 128), 8, 128)
                us, wn = ws.load(w_slab(self.dram["L2f_w2T"], 0, 8,
                                        ic * 128, 128), 8, 128)
                pg = self.pb[(ic % 2) * 2]
                pu = self.pb[(ic % 2) * 2 + 1]
                tg = [(gs[:, j * 128:(j + 1) * 128],
                       self.ch(self.B0, j, tb * 512, 512)) for j in range(8)]
                tu = [(us[:, j * 128:(j + 1) * 128],
                       self.ch(self.B0, j, tb * 512, 512)) for j in range(8)]
                mm_acc(P, pg[:, :], tg, waits=(("dw", wn),))
                mm_acc(P, pu[:, :], tu)
                P.wait("scalar", "t", "v")
                P.op("scalar", lambda e, pg=pg, ic=ic: e.activation(
                    t_sil[:], pg[:, :], AF.Silu, scale=col("l2_w1s", ic)),
                    inc="a")
                P.wait("vector", "a")
                P.op("vector", lambda e, pu=pu, ic=ic: e.tensor_scalar(
                    t2[:], pu[:, :], col("l2_w2s", ic), None, ALU.mult,
                    ALU.bypass))
                P.op("vector", lambda e, ic=ic: e.tensor_tensor(
                    self.B1[:, ic * 512:(ic + 1) * 512], t_sil[:], t2[:],
                    ALU.mult), inc="v")
                P.wait("tensor", "v")
                ws.release_last(2)
            self.l2_mid("L2f_h1T", self.B1, self.B2, "l2_h1s")
            self.l2_mid("L2f_h2T", self.B2, self.B1, "l2_h2s")
            P.wait("sync", "v", "a", "t", ("dx", self.x_spilled_at))
            P.dma("sync", self.SP[:].rearrange("p (c t) -> p c t", t=512),
                  xr[:, :, tsl], sem="dx")
            P.wait("sync", ("dx", self.s_spilled_at))
            P.dma("sync",
                  self.B2[:, 0:4096].rearrange("p (c t) -> p c t", t=512),
                  s3d[:, :, tsl], sem="dx")
            xold_dx = P.count("dx")
            P.wait("tensor", "a")
            ws3 = WStream(P, self.WR[:, 0:4096], 1024, 3)
            for ic in range(32):
                sl3, wn3 = ws3.load(w_slab(self.dram["L2f_w3T"], ic, 1,
                                           0, 1024), 1, 1024)
                for oc in range(8):
                    P.op("tensor", lambda e, sl3=sl3, oc=oc, ic=ic: e.matmul(
                        self.pb[oc][:, :], sl3[:, oc * 128:(oc + 1) * 128],
                        self.B1[:, ic * 512:(ic + 1) * 512],
                        start=(ic == 0), stop=(ic == 31),
                        skip_group_check=True),
                        inc="t" if (oc == 7) else None,
                        waits=(("dw", wn3),) if oc == 0 else ())
                ws3.release_last()
            P.wait("scalar", "t", ("dx", xold_dx))
            P.wait("vector", ("dx", xold_dx))
            for oc in range(8):
                P.op("scalar", lambda e, oc=oc: e.activation(
                    gtmp[:], self.pb[oc][:, :], AF.Silu,
                    scale=col("l2_w3s", oc)), inc="a")
                P.wait("vector", "a")
                s_ap = self.B2[:, oc * 512:(oc + 1) * 512]
                P.op("vector", lambda e, s_ap=s_ap: e.tensor_tensor(
                    gtmp[:], s_ap, gtmp[:], ALU.mult))
                xo = self.SP[:, oc * 512:(oc + 1) * 512]
                P.op("vector", lambda e, xo=xo: e.tensor_tensor(
                    xo, xo, gtmp[:], ALU.add), inc="v")
                P.wait("scalar", "v")
                P.wait("tensor", "v")
            P.wait("sync", "v")
            P.dma("sync", xr[:, :, tsl],
                  self.SP[:].rearrange("p (c t) -> p c t", t=512), sem="dx")
        P.wait("sync", ("dx", P.count("dx")))
        P.dma("sync", self.X[:].rearrange("p (c t) -> p c t", t=2048),
              xr, sem="dx")
        for eng in ("scalar", "vector", "tensor"):
            P.wait(eng, ("dx", P.count("dx")))

    def l2_mid(self, wname, src, dst, scname):
        P = self.P
        col = lambda name, j: self.bias[:, BCMAP[name] + j:
                                        BCMAP[name] + j + 1]
        P.wait("tensor", "v", "a")
        for og in range(4):
            ws = WStream(P, self.WR[:, 0:4096], 1024, 3)
            for ic in range(32):
                slab, wn = ws.load(w_slab(self.dram[wname], ic, 1,
                                          og * 1024, 1024), 1, 1024)
                for j8 in range(8):
                    P.op("tensor", lambda e, slab=slab, j8=j8, ic=ic:
                         e.matmul(self.pb[j8][:, :],
                                  slab[:, j8 * 128:(j8 + 1) * 128],
                                  src[:, ic * 512:(ic + 1) * 512],
                                  start=(ic == 0), stop=(ic == 31),
                                  skip_group_check=True),
                         inc="t" if j8 == 7 else None,
                         waits=(("dw", wn),) if j8 == 0 else ())
                ws.release_last()
            P.wait("scalar", "t")
            for j8 in range(8):
                oc = og * 8 + j8
                P.op("scalar", lambda e, j8=j8, oc=oc: e.activation(
                    dst[:, oc * 512:(oc + 1) * 512], self.pb[j8][:, :],
                    AF.Gelu, scale=col(scname, oc)), inc="a")
            P.wait("tensor", "a")

    def emit_layer3(self):
        P = self.P
        xn = [self.ch(self.B0, c) for c in range(8)]
        self.emit_ln(self.x_views, xn, self.B1)
        self.spill_x()
        Xb, SMb, SM = self.Xb, self.SMb, self.SM
        col = lambda name, j: self.bias[:, BCMAP[name] + j:
                                        BCMAP[name] + j + 1]
        P.wait("gpsimd", ("dx", self.x_spilled_at))
        wofs = {"L3w_wq": (0, 512), "L3w_wkg": (4096, 512),
                "L3w_wv": (8192, 1024), "L3w_wg": (16384, 1024),
                "L3w_wout": (24576, 1024)}
        for name, (base, ocols) in wofs.items():
            dst = Xb[:, base:base + 8 * ocols].rearrange(
                "p (c o) -> p c o", o=ocols)
            P.dma("gpsimd", dst, w_slab(self.dram[name], 0, 8, 0, ocols),
                  sem="dw")
        wload = P.count("dw")
        wtile = lambda name, kc, o0, on: Xb[:, wofs[name][0] +
                                            kc * wofs[name][1] + o0:
                                            wofs[name][0] + kc * wofs[name][1]
                                            + o0 + on]
        xc = lambda fc: SMb[:, fc * 128:(fc + 1) * 128]
        xcf = SM[:, 512:640]
        KGt = lambda j: SM[:, 640 + j * 128:640 + (j + 1) * 128]
        Gt = lambda j: SM[:, 1152 + j * 128:1152 + (j + 1) * 128]
        tmpa = SM[:, 1664:1792]
        tmpb = SM[:, 1792:1920]
        tmpc = SM[:, 1920:2048]
        B2 = self.B2
        qb = lambda j: B2[:, 7424 + j * 128:7424 + (j + 1) * 128]
        vtok = lambda o0, on: B2[:, o0:o0 + on]
        sgt = lambda fc: B2[:, 1024 + fc * 128:1024 + (fc + 1) * 128]
        kbf = lambda j: B2[:, 2048 + j * 128:2048 + (j + 1) * 128]
        qtl = lambda j: B2[:, 2560 + j * 128:2560 + (j + 1) * 128]
        kbar = lambda j: B2[:, 3072 + j * 128:3072 + (j + 1) * 128]
        khat_tok = lambda j: B2[:, 3584 + j * 128:3584 + (j + 1) * 128]
        A_bf = B2[:, 6144:6272]
        AT_bf = B2[:, 6272:6400]
        augsig = lambda h: B2[:, 6400 + h * 128:6400 + (h + 1) * 128]
        fused = lambda fc: B2[:, 5120 + fc * 128:5120 + (fc + 1) * 128]
        o_sb = SM[:, 512:640]
        o_sq = tmpa
        St = lambda j: self.SP[:, j * 128:(j + 1) * 128]
        Stb = lambda j: self.SPb[:, 1024 + j * 128:1024 + (j + 1) * 128]
        mcol = self.cols[:, 48:49]
        vcol = self.cols[:, 49:50]
        rcol = self.cols[:, 50:51]
        gccol = lambda j: self.cols[:, 54 + j:55 + j]
        awcol = self.cols[:, 52:54]
        P.wait("vector", ("dx", self.x_spilled_at))
        for j in range(4):
            P.op("vector", lambda e, j=j: e.memset(St(j), 0.0),
                 inc="v" if j == 3 else None)

        for c in range(16):
            t0 = c * 128
            P.wait("vector", "a")
            for fc in range(8):
                xnf = self.ch(self.B0, fc)
                if c == 0:
                    P.op("vector", lambda e, fc=fc, xnf=xnf: e.tensor_scalar(
                        xcf[:], xnf[:, 0:128], col("l3_conv", 3 * 8 + fc),
                        None, ALU.mult, ALU.bypass))
                    for tap in range(3):
                        d0 = 3 - tap
                        P.op("vector", lambda e, fc=fc, xnf=xnf, tap=tap,
                             d0=d0: e.scalar_tensor_tensor(
                                 xcf[:, d0:128], xnf[:, 0:128 - d0],
                                 col("l3_conv", tap * 8 + fc),
                                 xcf[:, d0:128], ALU.mult, ALU.add))
                else:
                    P.op("vector", lambda e, fc=fc, xnf=xnf, t0=t0:
                         e.tensor_scalar(
                             xcf[:], xnf[:, t0 - 3:t0 + 125],
                             col("l3_conv", 0 * 8 + fc), None, ALU.mult,
                             ALU.bypass))
                    for tap in range(1, 4):
                        P.op("vector", lambda e, fc=fc, xnf=xnf, tap=tap,
                             t0=t0: e.scalar_tensor_tensor(
                                 xcf[:], xnf[:, t0 - 3 + tap:t0 + 125 + tap],
                                 col("l3_conv", tap * 8 + fc), xcf[:],
                                 ALU.mult, ALU.add))
                P.op("vector", lambda e: e.nop() if False else None) \
                    if False else None
                P.wait("vector", "a")
                P.op("vector", lambda e, fc=fc: e.tensor_copy(
                    tmpb[:], xcf[:]), inc="v")
                P.wait("scalar", "v")
                P.op("scalar", lambda e, fc=fc: e.activation(
                    xc(fc), tmpb[:], AF.Silu), inc="a")
            P.wait("tensor", "a", ("dw", wload))
            for oc4 in range(4):
                tiles = [(wtile("L3w_wq", kc, oc4 * 128, 128), xc(kc))
                         for kc in range(8)]
                mm_acc(P, self.pb[oc4][:, 0:128], tiles)
            P.wait("scalar", "t")
            for oc4 in range(4):
                P.op("scalar", lambda e, oc4=oc4: e.activation(
                    qb(oc4), self.pb[oc4][:, 0:128], AF.Copy, scale=0.125),
                    inc="a" if oc4 == 3 else None)
            P.wait("tensor", "a")
            for oc4 in range(4):
                tiles = [(wtile("L3w_wkg", kc, oc4 * 128, 128), xc(kc))
                         for kc in range(8)]
                mm_acc(P, self.pb[4 + oc4][:, 0:128], tiles)
            P.wait("vector", "t")
            for oc4 in range(4):
                P.op("vector", lambda e, oc4=oc4: e.tensor_scalar(
                    KGt(oc4), self.pb[4 + oc4][:, 0:128], -40.0, None,
                    ALU.max, ALU.bypass), inc="v" if oc4 == 3 else None)
            P.wait("tensor", "v")
            for ot in range(2):
                tiles = [(xc(kc), wtile("L3w_wv", kc, ot * 512, 512))
                         for kc in range(8)]
                mm_acc(P, self.pb[ot][:, :], tiles)
            P.wait("scalar", "t")
            for ot in range(2):
                P.op("scalar", lambda e, ot=ot: e.activation(
                    vtok(ot * 512, 512), self.pb[ot][:, :], AF.Copy),
                    inc="a" if ot == 1 else None)
            P.wait("tensor", "a")
            for fc in range(8):
                tiles = [(wtile("L3w_wg", kc, fc * 128, 128), xc(kc))
                         for kc in range(8)]
                mm_acc(P, self.pb[2 + fc % 2][:, 0:128], tiles)
                P.wait("scalar", "t")
                P.op("scalar", lambda e, fc=fc: e.activation(
                    sgt(fc), self.pb[2 + fc % 2][:, 0:128], AF.Silu,
                    bias=col("l3_bg", fc)), inc="a")
                P.wait("tensor", "a")
            P.wait("scalar", "v")
            for j in range(4):
                P.op("scalar", lambda e, j=j: e.activation(
                    KGt(j), KGt(j), AF.Sigmoid))
                P.op("scalar", lambda e, j=j: e.activation(
                    KGt(j), KGt(j), AF.Ln), inc="a" if j == 3 else None)
            P.wait("vector", "a")
            for j in range(4):
                P.op("vector", lambda e, j=j: e.tensor_scalar(
                    KGt(j), KGt(j), 1.0 / 16.0, None, ALU.mult, ALU.bypass),
                    inc="v" if j == 3 else None)
            P.wait("scalar", "v")
            for j in range(4):
                P.op("scalar", lambda e, j=j: e.activation(
                    tmpa[:], KGt(j), AF.Exp), inc="a")
                P.wait("vector", "a")
                P.op("vector", lambda e, j=j: e.tensor_scalar(
                    kbf(j), tmpa[:], -1.0, 1.0, ALU.mult, ALU.add), inc="v")
                P.wait("scalar", "v")
            if c == 0:
                self.tap1("tap_l3xc", SMb[:, 0:1024])
                self.tap1("tap_l3q", B2[:, 7424:7936])
                self.tap1("tap_l3gk", SM[:, 640:1152])
                self.tap1("tap_l3v", B2[:, 0:1024])
                self.tap1("tap_l3sg", B2[:, 1024:2048])
                self.tap1("tap_l3k", B2[:, 2048:2560])
            for j in range(4):
                P.vop(lambda e, j=j: e.memset(tmpc[:], 0.0))
                P.vop(lambda e, j=j: e.tensor_tensor_scan(
                    Gt(j), KGt(j), tmpc[:], 0.0, ALU.add, ALU.add))
                P.wait("scalar", "v")
                P.aop(lambda e, j=j: e.activation(
                    tmpa[:], Gt(j), AF.Exp))
                P.aop(lambda e, j=j: e.activation(
                    tmpb[:], Gt(j), AF.Exp, scale=-1.0))
                P.wait("vector", "a")
                P.vop(lambda e, j=j: e.tensor_tensor(
                    qtl(j), qb(j), tmpa[:], ALU.mult))
                P.vop(lambda e, j=j: e.tensor_tensor(
                    kbar(j), kbf(j), tmpb[:], ALU.mult))
                P.wait("scalar", "v")
                P.aop(lambda e, j=j: e.activation(
                    gccol(j), Gt(j)[:, 127:128], AF.Exp))
                P.wait("vector", "a")
                P.vop(lambda e, j=j: e.tensor_scalar(
                    AT_bf[:], kbar(j), gccol(j), None, ALU.mult, ALU.bypass))
                P.wait("tensor", "v")
                P.op("tensor", lambda e, j=j: e.transpose(
                    self.pbb[6][:, 0:128], AT_bf[:], self.identb[:]),
                    inc="t")
                P.wait("vector", "t")
                P.vop(lambda e, j=j: e.tensor_copy(
                    khat_tok(j), self.pbb[6][:, 0:128]))
            for h in range(8):
                j, hb = h // 2, (h % 2) * 64
                vt = vtok(h * 128, 128)
                P.wait("tensor", "v", "a")
                P.op("tensor", lambda e, j=j, hb=hb: e.matmul(
                    self.pb[0][:, 0:128], qtl(j)[hb:hb + 64, :],
                    kbar(j)[hb:hb + 64, :], start=True, stop=True), inc="t")
                P.wait("vector", "t")
                P.vop(lambda e: e.tensor_tensor(
                    A_bf[:], self.pb[0][:, 0:128], self.glatri[:], ALU.mult))
                P.wait("tensor", "v")
                P.op("tensor", lambda e: e.transpose(
                    self.pbb[1][:, 0:128], A_bf[:], self.identb[:]), inc="t")
                P.wait("vector", "t")
                P.vop(lambda e: e.tensor_copy(
                    AT_bf[:], self.pbb[1][:, 0:128]))
                P.vop(lambda e, j=j: e.tensor_copy(Stb(j), St(j)))
                P.wait("tensor", "v")
                P.op("tensor", lambda e, vt=vt: e.matmul(
                    self.pb[2][:, 0:128], AT_bf[:], vt, start=True,
                    stop=False, skip_group_check=True))
                P.op("tensor", lambda e, j=j, hb=hb: e.matmul(
                    self.pb[2][:, 0:128], qtl(j)[hb:hb + 64, :],
                    Stb(j)[hb:hb + 64, :], start=False, stop=True,
                    skip_group_check=True), inc="t")
                P.op("tensor", lambda e, j=j, hb=hb, vt=vt: e.matmul(
                    self.pb[3][hb:hb + 64, 0:128] if hb else
                    self.pb[3][0:64, 0:128],
                    khat_tok(j)[:, hb:hb + 64], vt, start=True, stop=True,
                    tile_position=(0, hb), skip_group_check=True),
                    inc="t")
                P.wait("vector", "t", "a")
                P.vop(lambda e, j=j: e.tensor_tensor(
                    tmpb[:], qb(j), kbf(j), ALU.mult))
                P.vop(lambda e, j=j: e.tensor_scalar(
                    AT_bf[:], tmpb[:], col("l3_aug", j), None, ALU.mult,
                    ALU.bypass))
                P.wait("tensor", "v")
                P.op("tensor", lambda e: e.matmul(
                    self.pb[6][0:2, 0:128], self.sel2[:], AT_bf[:],
                    start=True, stop=True, skip_group_check=True), inc="t")
                P.wait("vector", "t")
                P.vop(lambda e: e.tensor_copy(
                    AT_bf[0:2, :], self.pb[6][0:2, 0:128]))
                P.wait("tensor", "v")
                P.op("tensor", lambda e: e.transpose(
                    self.pbb[6][:, 128:130], AT_bf[0:2, :],
                    self.identb[0:2, 0:2]), inc="t")
                P.wait("vector", "t")
                P.vop(lambda e, hb=hb: e.tensor_copy(
                    awcol[:, 0:1],
                    self.pb[6].bitcast(BF16)[:, 128 + (hb // 64):
                                             129 + (hb // 64)]))
                P.wait("scalar", "v")
                P.aop(lambda e, h=h: e.activation(tmpb[:], vt, AF.Copy))
                P.wait("vector", "a")
                P.vop(lambda e: e.tensor_scalar(
                    tmpb[:], tmpb[:], awcol[:, 0:1], None, ALU.mult,
                    ALU.bypass))
                P.wait("scalar", "v")
                P.aop(lambda e, h=h: e.activation(
                    tmpc[:], tmpb[:], AF.Sigmoid))
                P.wait("vector", "a")
                P.vop(lambda e, j=j, hb=hb: e.tensor_scalar(
                    St(j)[hb:hb + 64, :] if hb else St(j)[0:64, :],
                    St(j)[hb:hb + 64, :] if hb else St(j)[0:64, :],
                    gccol(j)[hb:hb + 64, :] if hb else gccol(j)[0:64, :],
                    None, ALU.mult, ALU.bypass))
                P.vop(lambda e, j=j, hb=hb: e.tensor_tensor(
                    St(j)[hb:hb + 64, :] if hb else St(j)[0:64, :],
                    St(j)[hb:hb + 64, :] if hb else St(j)[0:64, :],
                    self.pb[3][hb:hb + 64, 0:128] if hb else
                    self.pb[3][0:64, 0:128], ALU.add))
                P.vop(lambda e, h=h: e.tensor_tensor(
                    o_sb[:], self.pb[2][:, 0:128], tmpc[:], ALU.add))
                if c == 0 and h == 0:
                    self.tap1("tap_l3G", SM[:, 1152:1664])
                    self.tap1("tap_l3qt", B2[:, 2560:3072])
                    self.tap1("tap_l3kbar", B2[:, 3072:3584])
                    self.tap1("tap_l3A", A_bf[:])
                    self.tap1("tap_l3osb", o_sb[:])
                    self.tap1("tap_l3cols", self.cols[:, 0:64])
                P.vop(lambda e: e.reduce_sum(mcol[:], o_sb[:], axis=AX.X))
                P.vop(lambda e: e.tensor_scalar(
                    mcol[:], mcol[:], 1.0 / 128.0, None, ALU.mult,
                    ALU.bypass))
                P.wait("scalar", "v")
                P.op("scalar", lambda e: e.activation(
                    o_sq[:], o_sb[:], AF.Square), inc="a")
                P.wait("vector", "a")
                P.vop(lambda e: e.reduce_sum(vcol[:], o_sq[:], axis=AX.X))
                P.vop(lambda e: e.tensor_scalar(
                    vcol[:], vcol[:], 1.0 / 128.0, None, ALU.mult,
                    ALU.bypass))
                P.vop(lambda e: e.tensor_scalar(
                    tmpc[:, 0:1], mcol[:], mcol[:], None, ALU.mult,
                    ALU.bypass))
                P.vop(lambda e: e.tensor_tensor(
                    vcol[:], vcol[:], tmpc[:, 0:1], ALU.subtract))
                P.wait("scalar", "v")
                P.op("scalar", lambda e: e.activation(
                    vcol[:], vcol[:], AF.Ln, bias=self.cols[0:128, 24:25]))
                P.op("scalar", lambda e: e.activation(
                    rcol[:], vcol[:], AF.Exp, scale=-0.5), inc="a")
                P.wait("vector", "a")
                P.vop(lambda e: e.tensor_scalar(
                    tmpb[:], o_sb[:], mcol[:], rcol[:], ALU.subtract,
                    ALU.mult))
                P.wait("tensor", "v")
                P.op("tensor", lambda e: e.transpose(
                    self.pbb[4][:, 0:128], tmpb.bitcast(BF16)[:, 0:256][:, 0:128]
                    if False else tmpb[:], self.identb[:]), inc="t") \
                    if False else None
                P.vop(lambda e: e.tensor_copy(A_bf[:], tmpb[:]))
                P.wait("tensor", "v")
                P.op("tensor", lambda e: e.transpose(
                    self.pbb[4][:, 0:128], A_bf[:], self.identb[:]), inc="t")
                P.wait("vector", "t")
                P.vop(lambda e, h=h: e.tensor_tensor(
                    fused(h), self.pbb[4][:, 0:128], sgt(h), ALU.mult))
            if c == 0:
                self.tap1("tap_l3fused", B2[:, 5120:6144])
            P.wait("tensor", "v")
            for oc in range(8):
                tiles = [(wtile("L3w_wout", kc, oc * 128, 128), fused(kc))
                         for kc in range(8)]
                mm_acc(P, self.pb[4 + oc % 2][:, 0:128], tiles)
                P.wait("scalar", "t")
                P.op("scalar", lambda e, oc=oc, t0=t0: e.activation(
                    self.ch(self.B1, oc, t0, 128), self.pb[4 + oc % 2][:, 0:128],
                    AF.Copy), inc="a")
                P.wait("tensor", "a")
        self.tap("tap_l3s", [self.ch(self.B1, c) for c in range(8)])
        self.l3_ffn()

    def l3_ffn(self):
        P = self.P
        SMb = self.SMb
        col = lambda name, j: self.bias[:, BCMAP[name] + j:
                                        BCMAP[name] + j + 1]
        gwr = WStream(P, self.smallw[:, 64:128], 64, 1)
        gw, gwn = gwr.load(w_slab(self.dram["L3_ffn_w_router"], 0, 8, 0, 8),
                           8, 8)

        def logits(tt, pbank):
            tiles = [(self.ch(self.B0, c, tt * 128, 128),
                      gw[:, c * 8:(c + 1) * 8]) for c in range(8)]
            mm_acc(P, pbank[:, 0:8], tiles,
                   waits=(("dw", gwn),) if tt == 0 else ())
        CWb = self.gate_cwf(logits, thr=None)

        w1_all = [self.dram["L3_ffn_w1_shared"]] + \
            [self.dram[f"L3_ffn_w1r_{i}"] for i in range(8)]
        w2_all = [self.dram["L3_ffn_w2_shared"]] + \
            [self.dram[f"L3_ffn_w2r_{i}"] for i in range(8)]
        BW = self.SM[:, 0:512]
        t1 = self.SM[:, 512:1024]
        xr = self.xspill.rearrange("(c p) t -> p c t", p=128)
        for tb in range(4):
            tsl = slice(tb * 512, (tb + 1) * 512)
            for ei in range(9):
                if ei > 0:
                    self.bw_from_cw(CWb, ei - 1, tb * 4, 4, BW)
                P.wait("tensor", "a", "v")
                ws = WStream(P, self.WR[:, 0:4096], 1024, 4)
                for ic in range(32):
                    slab, wn = ws.load(w_slab(w1_all[ei], 0, 8,
                                              ic * 128, 128), 8, 128)
                    pg = self.pb[ic % 2]
                    tiles = [(slab[:, j * 128:(j + 1) * 128],
                              self.ch(self.B0, j, tb * 512, 512))
                             for j in range(8)]
                    mm_acc(P, pg[:, :], tiles, waits=(("dw", wn),))
                    P.wait("scalar", "t", "v")
                    P.op("scalar", lambda e, pg=pg, ic=ic: e.activation(
                        self.B2[:, ic * 512:(ic + 1) * 512], pg[:, :],
                        AF.Silu), inc="a")
                    P.wait("tensor", "a")
                    ws.release_last()
                P.wait("tensor", "a")
                ws2 = WStream(P, self.WR[:, 0:4096], 1024, 3)
                for ic in range(32):
                    slab, wn = ws2.load(w_slab(w2_all[ei], ic, 1, 0, 1024),
                                        1, 1024)
                    for oc in range(8):
                        P.op("tensor", lambda e, slab=slab, oc=oc, ic=ic:
                             e.matmul(self.pb[oc][:, :],
                                      slab[:, oc * 128:(oc + 1) * 128],
                                      self.B2[:, ic * 512:(ic + 1) * 512],
                                      start=(ic == 0), stop=(ic == 31),
                                      skip_group_check=True),
                             inc="t" if oc == 7 else None,
                             waits=(("dw", wn),) if oc == 0 else ())
                    ws2.release_last()
                P.wait("vector", "t")
                for oc in range(8):
                    acc = self.SP[:, oc * 512:(oc + 1) * 512]
                    if ei == 0:
                        P.op("vector", lambda e, acc=acc, oc=oc:
                             e.tensor_copy(acc, self.pb[oc][:, :]), inc="v")
                    else:
                        P.op("vector", lambda e, oc=oc: e.tensor_tensor(
                            t1[:], self.pb[oc][:, :], BW[:], ALU.mult))
                        P.op("vector", lambda e, acc=acc: e.tensor_tensor(
                            acc, acc, t1[:], ALU.add), inc="v")
                P.wait("tensor", "v")
            if tb == 1:
                self.tap1("tap_l3acc", self.SP[:, 0:4096])
                self.tap1("tap_l3cw", CWb[:, 0:128])
            P.wait("sync", "v", "a", "t", ("dx", self.x_spilled_at))
            P.dma("sync", self.X[:, 0:4096].rearrange("p (c t) -> p c t",
                                                      t=512),
                  xr[:, :, tsl], sem="dx")
            xold_dx = P.count("dx")
            gtmp = self.SM[:, 1024:1536]
            P.wait("scalar", "v", ("dx", xold_dx))
            P.wait("vector", ("dx", xold_dx))
            if tb == 1:
                self.tap1("tap_l3xold", self.X[:, 0:4096])
            for oc in range(8):
                acc = self.SP[:, oc * 512:(oc + 1) * 512]
                P.op("scalar", lambda e, acc=acc: e.activation(
                    gtmp[:], acc, AF.Silu), inc="a")
                P.wait("vector", "a")
                s_ap = self.ch(self.B1, oc, tb * 512, 512)
                P.op("vector", lambda e, s_ap=s_ap: e.tensor_tensor(
                    gtmp[:], s_ap, gtmp[:], ALU.mult))
                xo = self.X[:, oc * 512:(oc + 1) * 512]
                P.op("vector", lambda e, xo=xo: e.tensor_tensor(
                    xo, xo, gtmp[:], ALU.add), inc="v")
                P.wait("scalar", "v")
            P.wait("sync", "v")
            P.dma("sync", xr[:, :, tsl],
                  self.X[:, 0:4096].rearrange("p (c t) -> p c t", t=512),
                  sem="dx")
        P.wait("sync", ("dx", P.count("dx")))
        P.dma("sync", self.X[:].rearrange("p (c t) -> p c t", t=2048),
              xr, sem="dx")
        for eng in ("scalar", "vector", "tensor"):
            P.wait(eng, ("dx", P.count("dx")))

    def build(self, layers=(0,), debug_taps=()):
        nc = self.nc
        self.xin = self.din("xT", (1024, 2048))
        self.xout = nc.dram_tensor("xoutT", [1024, 2048], F32,
                                   kind="ExternalOutput").ap()
        self.din("hc_rope_cos", (128, 2048), BF16)
        self.din("hc_rope_sin", (128, 2048), BF16)
        self.din("hc_rot", (128, 128), BF16)
        self.din("hc_biascols", (128, 384))
        for name, shape in HOST_INPUT_SHAPES.items():
            self.din(name, shape)
        self.debug_taps = {}
        for tname, shape in debug_taps:
            self.debug_taps[tname] = nc.dram_tensor(
                tname, list(shape), F32, kind="ExternalOutput").ap()

        self.alloc()
        P = self.P
        block = self.stack.enter_context(nc.Block())
        self.emit_consts()

        xr = self.xin.rearrange("(c p) t -> p c t", p=128)
        P.dma("sync", self.X[:].rearrange("p (c t) -> p c t", t=2048), xr,
              sem="dx")
        for eng in ("scalar", "vector", "tensor", "gpsimd"):
            P.wait(eng, "dx")

        self.x_views = [self.xv(c) for c in range(8)]
        for li in layers:
            getattr(self, f"emit_layer{li}")()

        P.wait("sync", "v", "a", "t")
        outr = self.xout.rearrange("(c p) t -> p c t", p=128)
        P.dma("sync", outr, self.X[:].rearrange("p (c t) -> p c t", t=2048),
              sem="ds")
        P.wait("sync", "ds")
        P.emit(block)
        self.stack.close()
        return nc



BCMAP = {
    "l2_w1s": 0, "l2_w2s": 32, "l2_h1s": 64, "l2_h2s": 96, "l2_w3s": 128,
    "l1_bq": 136, "l1_bk": 144, "l1_bv": 152, "l1_bo": 160,
    "l1_beq": 168, "l1_bek": 176, "l1_lng": 184, "l1_lnb": 192,
    "l1_ampb": 200, "l1_phb": 201, "l1_bp": 202, "l1_be": 210,
    "l3_bg": 274, "l3_conv": 282, "l3_aug": 314,
    "l1_mixb": 320, "l1_mixbr": 328, "eps": 336, "l1_bqs": 344,
}

HOST_INPUT_SHAPES = {
    "L0w_WqT": (1024, 1024), "L0w_WkT": (1024, 1024), "L0w_WvT": (1024, 1024),
    "L0w_WoT": (1024, 1024),
    "L0_ffn_gate_w": (1024, 8),
    "L1w_WqT": (1024, 1024), "L1w_WkT": (1024, 1024),
    "L1w_WvT": (1024, 1024), "L1w_WoT": (1024, 1024),
    "L1w_WeqT": (1024, 1024), "L1w_WekT": (1024, 1024),
    "L1w_ampT": (1024, 16), "L1w_phT": (1024, 16),
    "L1w_mixT": (1024, 64), "L1w_mixrT": (1024, 64),
    "L1w_WpT": (1024, 1024),
    "L2w_WqT": (1024, 1024), "L2w_WkT": (1024, 1024),
    "L2w_WvT": (1024, 1024), "L2w_WoT": (1024, 1024),
    "L2f_w1T": (1024, 4096), "L2f_w2T": (1024, 4096),
    "L2f_h1T": (4096, 4096), "L2f_h2T": (4096, 4096),
    "L2f_w3T": (4096, 1024),
    "L3w_wq": (1024, 512), "L3w_wkg": (1024, 512), "L3w_wv": (1024, 1024),
    "L3w_wg": (1024, 1024), "L3w_wout": (1024, 1024),
    "L3_ffn_w1_shared": (1024, 4096), "L3_ffn_w2_shared": (4096, 1024),
    "L3_ffn_w_router": (1024, 8),
    "L0_ffn_base_wgu": (1024, 5632), "L0_ffn_base_wd": (2816, 1024),
}
for _i in range(8):
    HOST_INPUT_SHAPES[f"L0_ffn_exp_wgu_{_i}"] = (1024, 5632)
    HOST_INPUT_SHAPES[f"L0_ffn_exp_wd_{_i}"] = (2816, 1024)
    HOST_INPUT_SHAPES[f"L3_ffn_w1r_{_i}"] = (1024, 4096)
    HOST_INPUT_SHAPES[f"L3_ffn_w2r_{_i}"] = (4096, 1024)
    HOST_INPUT_SHAPES[f"L1f_WeT_{_i}"] = (1024, 1024)
    HOST_INPUT_SHAPES[f"L1f_Vg_{_i}"] = (1024, 64)


def pack_cols(vec, ncols):
    return np.ascontiguousarray(vec.reshape(ncols, 128).T)


def host_prep(inputs):
    C = np.ascontiguousarray
    d = {}
    bc = np.zeros((128, 384), np.float32)
    bc[:, BCMAP["l2_w1s"]:BCMAP["l2_w1s"] + 32] = pack_cols(inputs["L2_ffn_w1_s"], 32)
    bc[:, BCMAP["l2_w2s"]:BCMAP["l2_w2s"] + 32] = pack_cols(inputs["L2_ffn_w2_s"], 32)
    bc[:, BCMAP["l2_h1s"]:BCMAP["l2_h1s"] + 32] = pack_cols(inputs["L2_ffn_h1_s"], 32)
    bc[:, BCMAP["l2_h2s"]:BCMAP["l2_h2s"] + 32] = pack_cols(inputs["L2_ffn_h2_s"], 32)
    bc[:, BCMAP["l2_w3s"]:BCMAP["l2_w3s"] + 8] = pack_cols(inputs["L2_ffn_w3_s"], 8)
    for nm, key in (("l1_bq", "L1_attn_bq"), ("l1_bk", "L1_attn_bk"),
                    ("l1_bv", "L1_attn_bv"), ("l1_bo", "L1_attn_bo"),
                    ("l1_beq", "L1_attn_beq"), ("l1_bek", "L1_attn_bek"),
                    ("l1_lng", "L1_attn_ln_g"), ("l1_lnb", "L1_attn_ln_b"),
                    ("l1_bp", "L1_ffn_bp")):
        bc[:, BCMAP[nm]:BCMAP[nm] + 8] = pack_cols(inputs[key], 8)
    bc[0:16, BCMAP["l1_ampb"]] = inputs["L1_attn_amp_b"]
    bc[0:16, BCMAP["l1_phb"]] = inputs["L1_attn_ph_b"]
    for e_ in range(8):
        bc[:, BCMAP["l1_be"] + e_ * 8:BCMAP["l1_be"] + e_ * 8 + 8] = \
            pack_cols(inputs["L1_ffn_be"][e_], 8)
    bc[:, BCMAP["l3_bg"]:BCMAP["l3_bg"] + 8] = pack_cols(inputs["L3_attn_bg"], 8)
    for tap in range(4):
        bc[:, BCMAP["l3_conv"] + tap * 8:BCMAP["l3_conv"] + tap * 8 + 8] = \
            pack_cols(inputs["L3_attn_conv_w"][:, tap], 8)
    bc[:, BCMAP["l3_aug"]:BCMAP["l3_aug"] + 4] = \
        pack_cols(inputs["L3_attn_aug_balance"], 4)
    d["hc_biascols"] = bc
    d["L1w_WqT"] = C(inputs["L1_attn_Wq"].T)
    d["L1w_WkT"] = C(inputs["L1_attn_Wk"].T)
    d["L1w_WvT"] = C(inputs["L1_attn_Wv"].T)
    d["L1w_WoT"] = C(inputs["L1_attn_Wo"].T)
    d["L1w_WeqT"] = C(inputs["L1_attn_Weq"].T)
    d["L1w_WekT"] = C(inputs["L1_attn_Wek"].T)
    d["L1w_ampT"] = C(inputs["L1_attn_amp_w"].T)
    d["L1w_phT"] = C(inputs["L1_attn_ph_w"].T)
    d["L1w_mixT"] = C(inputs["L1_attn_mix_w"].transpose(0, 2, 1).reshape(1024, 64))
    d["L1w_mixrT"] = C(np.roll(inputs["L1_attn_mix_w"], 1, axis=1)
                       .transpose(0, 2, 1).reshape(1024, 64))
    d["L1w_WpT"] = C(inputs["L1_ffn_Wp"].T)
    for i in range(8):
        d[f"L1f_WeT_{i}"] = C(inputs["L1_ffn_We"][i].T)
        d[f"L1f_Vg_{i}"] = C(inputs["L1_ffn_Vg"][i])
    mixb = inputs["L1_attn_mix_b"]
    mb = np.zeros((128, 8), np.float32)
    mbr = np.zeros((128, 8), np.float32)
    mixbr = np.roll(mixb, 1, axis=1)
    for h in range(16):
        hb = (h % 2) * 64
        mb[hb:hb + 64, h // 2] = mixb[h]
        mbr[hb:hb + 64, h // 2] = mixbr[h]
    bc[:, BCMAP["l1_mixb"]:BCMAP["l1_mixb"] + 8] = mb
    bc[:, BCMAP["l1_mixbr"]:BCMAP["l1_mixbr"] + 8] = mbr
    beq = inputs["L1_attn_beq"]
    bq2 = np.zeros((128, 8), np.float32)
    for h in range(16):
        hb = (h % 2) * 64
        bq2[hb:hb + 64, h // 2] = beq[h * 64:(h + 1) * 64]
    bc[:, BCMAP["l1_beq"]:BCMAP["l1_beq"] + 8] = bq2
    d["L2w_WqT"] = C(inputs["L2_attn_Wq"].T)
    d["L2w_WkT"] = C(inputs["L2_attn_Wk"].T)
    d["L2w_WvT"] = C(inputs["L2_attn_Wv"].T)
    d["L2w_WoT"] = C(inputs["L2_attn_Wo"].T)
    d["L2f_w1T"] = C(inputs["L2_ffn_w1_w"].T)
    d["L2f_w2T"] = C(inputs["L2_ffn_w2_w"].T)
    d["L2f_h1T"] = C(inputs["L2_ffn_h1_w"].T)
    d["L2f_h2T"] = C(inputs["L2_ffn_h2_w"].T)
    d["L2f_w3T"] = C(inputs["L2_ffn_w3_w"].T)
    d["L3w_wq"] = C(inputs["L3_attn_wq"])
    d["L3w_wkg"] = C(inputs["L3_attn_wkg"])
    d["L3w_wv"] = C(inputs["L3_attn_wv"])
    d["L3w_wg"] = C(inputs["L3_attn_wg"])
    d["L3w_wout"] = C(inputs["L3_attn_wout"])
    d["L3_ffn_w1_shared"] = C(inputs["L3_ffn_w1_shared"])
    d["L3_ffn_w2_shared"] = C(inputs["L3_ffn_w2_shared"])
    d["L3_ffn_w_router"] = C(inputs["L3_ffn_w_router"])
    for i in range(8):
        d[f"L3_ffn_w1r_{i}"] = C(inputs["L3_ffn_w1_routed"][i])
        d[f"L3_ffn_w2r_{i}"] = C(inputs["L3_ffn_w2_routed"][i])
    d["L0w_WqT"] = C(inputs["L0_attn_Wq"].T)
    d["L0w_WkT"] = C(inputs["L0_attn_Wk"].T)
    d["L0w_WvT"] = C(inputs["L0_attn_Wv"].T)
    d["L0w_WoT"] = C(inputs["L0_attn_Wo"].T)
    d["L0_ffn_gate_w"] = C(inputs["L0_ffn_gate_w"])
    d["L0_ffn_base_wgu"] = C(inputs["L0_ffn_base_wgu"])
    d["L0_ffn_base_wd"] = C(inputs["L0_ffn_base_wd"])
    for i in range(8):
        d[f"L0_ffn_exp_wgu_{i}"] = C(inputs["L0_ffn_exp_wgu"][i])
        d[f"L0_ffn_exp_wd_{i}"] = C(inputs["L0_ffn_exp_wd"][i])
    return d


def host_consts():
    inv_freq = (1.0 / 10000.0 ** (np.arange(0, 64, 2) / 64.0)).astype(np.float64)
    ang = np.arange(2048)[None, :] * inv_freq[:, None]
    cos = np.repeat(np.cos(ang), 2, axis=0).astype(np.float32)
    sin = np.repeat(np.sin(ang), 2, axis=0).astype(np.float32)
    cos = np.tile(cos, (2, 1))
    sin = np.tile(sin, (2, 1))
    rot = np.zeros((128, 128), np.float32)
    for i in range(64):
        rot[2 * i + 1, 2 * i] = -1.0
        rot[2 * i, 2 * i + 1] = 1.0
    bf = ml_dtypes.bfloat16
    return {"hc_rope_cos": cos.astype(bf), "hc_rope_sin": sin.astype(bf),
            "hc_rot": rot.astype(bf)}


_BUILD_CACHE = {}


def kernel(**inputs):
    if "nc" not in _BUILD_CACHE:
        mk = MK()
        _BUILD_CACHE["nc"] = mk.build(layers=(0, 1, 2, 3))
    nc = _BUILD_CACHE["nc"]
    d = host_prep(inputs)
    d.update(host_consts())
    x = np.asarray(inputs["L0_attn_x"])
    in_maps = []
    for b in range(8):
        db = dict(d)
        db["xT"] = np.ascontiguousarray(x[b].T)
        in_maps.append(db)
    res = run_bass_kernel_spmd(nc, in_maps, core_ids=list(range(8)))
    out = np.stack([np.asarray(res.results[b]["xoutT"]).T for b in range(8)])
    return np.ascontiguousarray(out.astype(np.float32))
